# version 13
# baseline (speedup 1.0000x reference)
import numpy as np
import ml_dtypes
from contextlib import ExitStack

import concourse.bass as bass
import concourse.mybir as mybir
from concourse.alu_op_type import AluOpType as OP
from concourse.bass_utils import run_bass_kernel_spmd

F32 = mybir.dt.float32
BF16 = mybir.dt.bfloat16
AF = mybir.ActivationFunctionType
AX = mybir.AxisListType

B, T, D = 4, 4096, 2048
DH, NG, NH = 128, 4, 16
HID = 8192
NT = T // 128
NSLOT = 16
NKC = D // 128
NHC = HID // 128
EPS_LN, EPS_RMS = 1e-5, 1e-6
LNMULT = float(-0.25 * np.log(128.0))
NEG = -1e30
CONV_COLS = 1024


def own_tiles_for_parity(p):
    tiles = []
    for k in range(8):
        tiles += [4 * k, 4 * k + 3] if p == 0 else [4 * k + 1, 4 * k + 2]
    return tiles


def slot_extent(s):
    k = s // 2
    return 4 * k + 2 if s % 2 == 0 else 4 * k + 4


class Sem:
    def __init__(self, stack, nc, name):
        self.h = stack.enter_context(nc.semaphore(name))
        self.n = 0
        self.name = name


class Builder:
    def __init__(self):
        self.nc = bass.Bass("TRN2", target_bir_lowering=False)
        self.stack = ExitStack()
        self.lastw = {}

    def sem(self, name):
        return Sem(self.stack, self.nc, name)

    def sb(self, name, shape, dt):
        return self.stack.enter_context(self.nc.sbuf_tensor(name, shape, dt))

    def wait(self, eng, sem, n=None):
        n = sem.n if n is None else n
        if n <= 0:
            return
        key = (id(eng), sem.name)
        if self.lastw.get(key, -1) >= n:
            return
        eng.wait_ge(sem.h, n)
        self.lastw[key] = n


def build_program(dbg=False):
    b = Builder()
    nc = b.nc

    xb = nc.dram_tensor("xb", [T, D], F32, kind="ExternalInput")
    xq = nc.dram_tensor("xq", [NSLOT * 128, D], F32, kind="ExternalInput")
    wq_r = nc.dram_tensor("wq_r", [128, 4 * NKC * 512], F32, kind="ExternalInput")
    wk_r = nc.dram_tensor("wk_r", [128, NKC * 512], F32, kind="ExternalInput")
    wv_r = nc.dram_tensor("wv_r", [128, NKC * 512], F32, kind="ExternalInput")
    wo_r = nc.dram_tensor("wo_r", [128, NKC * 2048], F32, kind="ExternalInput")
    wg_r = nc.dram_tensor("wg_r", [128, NHC * 2048], F32, kind="ExternalInput")
    wu_r = nc.dram_tensor("wu_r", [128, NHC * 2048], F32, kind="ExternalInput")
    wd_r = nc.dram_tensor("wd_r", [128, 16 * 8192], F32, kind="ExternalInput")
    gqb = nc.dram_tensor("gqb", [128, D], F32, kind="ExternalInput")
    gkb = nc.dram_tensor("gkb", [128, 512], F32, kind="ExternalInput")
    kcos = nc.dram_tensor("kcos", [128, NT * 64], F32, kind="ExternalInput")
    ksin = nc.dram_tensor("ksin", [128, NT * 64], F32, kind="ExternalInput")
    qcos = nc.dram_tensor("qcos", [128, NSLOT * 64], F32, kind="ExternalInput")
    qsin = nc.dram_tensor("qsin", [128, NSLOT * 64], F32, kind="ExternalInput")
    ident = nc.dram_tensor("ident", [128, 128], BF16, kind="ExternalInput")
    masks = nc.dram_tensor("masks", [128, 4 * 128], F32, kind="ExternalInput")
    consts = nc.dram_tensor("consts", [128, 4], F32, kind="ExternalInput")
    out = nc.dram_tensor("out", [NSLOT * 128, D], F32, kind="ExternalOutput")
    wg_s = nc.dram_tensor("wg_s", [128, NHC * 2048], BF16, kind="Internal")
    wu_s = nc.dram_tensor("wu_s", [128, NHC * 2048], BF16, kind="Internal")
    wd_s = nc.dram_tensor("wd_s", [128, 16 * 8192], BF16, kind="Internal")
    wo_s = nc.dram_tensor("wo_s", [128, NKC * 2048], BF16, kind="Internal")
    res1 = nc.dram_tensor("res1", [NSLOT * 128, D], F32,
                          kind=("ExternalOutput" if dbg == 2 else "Internal"))
    if dbg == 2:
        dbg_gated = nc.dram_tensor("dbg_gated", [128, 32 * 512], F32, kind="ExternalOutput")
        dbg_xnt = nc.dram_tensor("dbg_xnt", [128, NKC * 512], F32, kind="ExternalOutput")
    if dbg:
        dbg_kt = nc.dram_tensor("dbg_kt", [128, NG * T], F32, kind="ExternalOutput")
        dbg_v = nc.dram_tensor("dbg_v", [128, NT * NG * 128], F32, kind="ExternalOutput")
        dbg_ot = nc.dram_tensor("dbg_ot", [128, NH * NSLOT * 128], F32, kind="ExternalOutput")

    s_ld = b.sem("s_ld")
    s_w = b.sem("s_w")
    s_out = b.sem("s_out")
    s_x = [b.sem("s_x0"), b.sem("s_x1")]
    s_kc = [b.sem("s_kc0"), b.sem("s_kc1")]
    s_wq = b.sem("s_wq")
    s_wg = [b.sem("s_wg0"), b.sem("s_wg1")]
    s_wu = [b.sem("s_wu0"), b.sem("s_wu1")]
    s_wd = [b.sem("s_wd0"), b.sem("s_wd1")]
    s_r1 = [b.sem("s_r10"), b.sem("s_r11")]
    s_st = [b.sem("s_st0"), b.sem("s_st1")]
    s_cr = [b.sem("s_cr0"), b.sem("s_cr1")]
    s_cw = [b.sem("s_cw0"), b.sem("s_cw1")]
    s_pe = b.sem("s_pe")
    s_dve = b.sem("s_dve")
    s_act = b.sem("s_act")

    def inc(sem, ins, k=1):
        ins.then_inc(sem.h, k)
        sem.n += k

    ident_sb = b.sb("ident_sb", [128, 128], BF16)
    masks_sb = b.sb("masks_sb", [128, 4 * 128], F32)
    consts_sb = b.sb("consts_sb", [128, 4], F32)
    nc.sync.dma_start(ident_sb[:, :], ident[:, :]).then_inc(s_ld.h, 16); s_ld.n += 16
    nc.sync.dma_start(masks_sb[:, :], masks[:, :]).then_inc(s_ld.h, 16); s_ld.n += 16
    nc.sync.dma_start(consts_sb[:, :], consts[:, :]).then_inc(s_ld.h, 16); s_ld.n += 16
    C_EPSLN = consts_sb[:, 0:1]
    C_EPSRMS = consts_sb[:, 1:2]
    C_LNMULT = consts_sb[:, 2:3]

    conv_jobs = []
    for srcw, dstw, ncols in ((wg_r, wg_s, NHC * 2048), (wu_r, wu_s, NHC * 2048),
                              (wd_r, wd_s, 16 * 8192), (wo_r, wo_s, NKC * 2048)):
        for c0 in range(0, ncols, CONV_COLS):
            conv_jobs.append((srcw, dstw, c0))
    conv_state = {"i": 0}

    def emit_conversions(conv_buf, k):
        for _ in range(k):
            i = conv_state["i"]
            if i >= len(conv_jobs):
                return
            conv_state["i"] += 1
            srcw, dstw, c0 = conv_jobs[i]
            par = i % 2
            b.wait(nc.gpsimd, s_cw[par])
            nc.gpsimd.dma_start(conv_buf[par][:, :],
                                srcw[:, c0:c0 + CONV_COLS]).then_inc(s_cr[par].h, 16)
            s_cr[par].n += 16
            b.wait(nc.sync, s_cr[par])
            nc.sync.dma_start(dstw[:, c0:c0 + CONV_COLS],
                              conv_buf[par][:, :]).then_inc(s_cw[par].h, 16)
            s_cw[par].n += 16

    def emit_ln(x_ap, xn_ap, bn_tmp, mnv, rstdc, xn_rd):
        for c4 in range(4):
            ins = nc.vector.bn_stats(bn_tmp[:, c4 * 6:(c4 + 1) * 6],
                                     x_ap[:, c4 * 512:(c4 + 1) * 512])
        inc(s_dve, ins)
        b.wait(nc.vector, s_dve)
        inc(s_dve, nc.vector.bn_aggr(mnv[:, :], bn_tmp[:, :]))
        b.wait(nc.scalar, s_dve)
        inc(s_act, nc.scalar.activation(rstdc[:, 0:1], mnv[:, 1:2], AF.Ln, bias=C_EPSLN))
        b.wait(nc.scalar, s_act)
        inc(s_act, nc.scalar.activation(rstdc[:, 1:2], rstdc[:, 0:1], AF.Exp, scale=-0.5))
        b.wait(nc.vector, s_act)
        b.wait(nc.vector, s_pe, xn_rd)
        ins = nc.vector.tensor_scalar(xn_ap, x_ap, mnv[:, 0:1], rstdc[:, 1:2],
                                      op0=OP.subtract, op1=OP.mult)
        inc(s_dve, ins)
        return s_dve.n

    def emit_rope(src_f32, dst_bf16, cosb, sinb, ngrp, scr, tr_rd):
        sv = src_f32.rearrange("p (g d) -> p g d", d=128)
        dv = dst_bf16.rearrange("p (g d) -> p g d", d=128)
        sA = scr[:, 0:256].rearrange("p (g d) -> p g d", d=64)[:, 0:ngrp, :]
        sB = scr[:, 256:512].rearrange("p (g d) -> p g d", d=64)[:, 0:ngrp, :]
        sC = scr[:, 512:768].rearrange("p (g d) -> p g d", d=64)[:, 0:ngrp, :]
        sD = scr[:, 768:1024].rearrange("p (g d) -> p g d", d=64)[:, 0:ngrp, :]
        ev = sv[:, :, 0:64]
        od = sv[:, :, 64:128]
        nc.vector.tensor_tensor(sA, ev, cosb, op=OP.mult)
        nc.vector.tensor_tensor(sB, od, sinb, op=OP.mult)
        nc.vector.tensor_tensor(sC, ev, sinb, op=OP.mult)
        ins = nc.vector.tensor_tensor(sD, od, cosb, op=OP.mult)
        inc(s_dve, ins)
        b.wait(nc.vector, s_dve)
        b.wait(nc.vector, s_pe, tr_rd)
        nc.vector.tensor_tensor(dv[:, :, 0:64], sA, sB, op=OP.subtract)
        ins = nc.vector.tensor_tensor(dv[:, :, 64:128], sC, sD, op=OP.add)
        inc(s_dve, ins)
        return s_dve.n

    with ExitStack() as S1:
        QT_OT = S1.enter_context(nc.sbuf_tensor("QT_OT", [128, NH * NSLOT * 128], BF16))
        QT_v = QT_OT[:, :].rearrange("p (h n) -> p h n", h=NH)
        with ExitStack() as S2:
            KT = S2.enter_context(nc.sbuf_tensor("KT", [128, NG * T], BF16))
            V = S2.enter_context(nc.sbuf_tensor("V", [128, NT * NG * 128], BF16))

            with ExitStack() as p1:
                sb1 = lambda n, s, d: p1.enter_context(nc.sbuf_tensor(n, s, d))
                ps1 = lambda n, s: p1.enter_context(nc.psum_tensor(n, s, F32))
                kc_t = [sb1(f"kc_t{i}", [128, 64], F32) for i in range(2)]
                ks_t = [sb1(f"ks_t{i}", [128, 64], F32) for i in range(2)]
                gkb_sb = sb1("gkb_sb", [128, 512], F32)
                wk_sb = sb1("wk_sb", [128, NKC * 512], BF16)
                wv_sb = sb1("wv_sb", [128, NKC * 512], BF16)
                x_buf = [sb1(f"x_buf{i}", [128, D], F32) for i in range(2)]
                xn_buf = [sb1(f"xn_buf{i}", [128, D], BF16) for i in range(2)]
                xnT_t = sb1("xnT_t", [128, NKC * 128], BF16)
                scr = sb1("scr", [128, 1024], F32)
                kn_scr = sb1("kn_scr", [128, 512], F32)
                krope = [sb1(f"krope{i}", [128, 512], BF16) for i in range(2)]
                sq_scr = sb1("sq_scr", [128, 512], BF16)
                bn_tmp = sb1("bn_tmp", [128, 24], F32)
                mnv = sb1("mnv", [128, 2], F32)
                rstdc = sb1("rstdc", [128, 12], F32)
                conv_buf = [sb1(f"cvb{i}", [128, CONV_COLS], BF16) for i in range(2)]
                ps_t = [ps1(f"ps_t{i}", [128, 512]) for i in range(2)]
                ps_k = [ps1(f"ps_k{i}", [128, 512]) for i in range(2)]
                ps_v = [ps1(f"ps_v{i}", [128, 512]) for i in range(2)]

                nc.sync.dma_start(gkb_sb[:, :], gkb[:, :]).then_inc(s_ld.h, 16); s_ld.n += 16
                nc.gpsimd.dma_start(wk_sb[:, :], wk_r[:, :]).then_inc(s_w.h, 16); s_w.n += 16
                nc.gpsimd.dma_start(wv_sb[:, :], wv_r[:, :]).then_inc(s_w.h, 16); s_w.n += 16
                tbl_ld, tbl_w = s_ld.n, s_w.n
                for eng in (nc.tensor, nc.vector, nc.scalar, nc.gpsimd, nc.sync):
                    b.wait(eng, s_ld, tbl_ld)

                x_rd = [0, 0]
                kct_rd = [0, 0]
                xn_rd_l = [0, 0]
                ktr_rd = [0, 0]
                ps_t_rd = [0, 0]
                ps_k_rd = [0, 0]
                ps_v_rd = [0, 0]
                xnT_rd = 0
                kv_marks = {}
                kr_marks = {}

                def s_load(t):
                    xbuf = t % 2
                    b.wait(nc.sync, s_dve, x_rd[xbuf])
                    nc.sync.dma_start(x_buf[xbuf][:, :],
                                      xb[t * 128:(t + 1) * 128, :]).then_inc(s_x[xbuf].h, 16)
                    s_x[xbuf].n += 16
                    b.wait(nc.sync, s_dve, kct_rd[xbuf])
                    nc.sync.dma_start(kc_t[xbuf][:, :], kcos[:, t * 64:(t + 1) * 64]).then_inc(s_kc[xbuf].h, 16)
                    nc.sync.dma_start(ks_t[xbuf][:, :], ksin[:, t * 64:(t + 1) * 64]).then_inc(s_kc[xbuf].h, 16)
                    s_kc[xbuf].n += 32

                def s_front(t):
                    nonlocal xnT_rd
                    xbuf = t % 2
                    b.wait(nc.vector, s_x[xbuf])
                    xn_mark = emit_ln(x_buf[xbuf][:, :], xn_buf[xbuf][:, :], bn_tmp, mnv, rstdc, xn_rd_l[xbuf])
                    x_rd[xbuf] = s_dve.n
                    b.wait(nc.tensor, s_dve, xn_mark)
                    for bi in range(4):
                        pb = bi % 2
                        b.wait(nc.tensor, s_act, ps_t_rd[pb])
                        b.wait(nc.tensor, s_pe, xnT_rd)
                        for j in range(4):
                            kc = bi * 4 + j
                            ins = nc.tensor.matmul(ps_t[pb][:, j * 128:(j + 1) * 128],
                                                   xn_buf[xbuf][:, kc * 128:(kc + 1) * 128],
                                                   ident_sb[:, :], start=True, stop=True)
                        inc(s_pe, ins)
                        b.wait(nc.scalar, s_pe)
                        dst = xnT_t[:, :].rearrange("p (kc n) -> p kc n", n=128)[:, bi * 4:(bi + 1) * 4, :]
                        srcv = ps_t[pb][:, :].rearrange("p (j n) -> p j n", n=128)
                        inc(s_act, nc.scalar.activation(dst, srcv, AF.Copy))
                        ps_t_rd[pb] = s_act.n
                    xn_rd_l[xbuf] = s_pe.n
                    xnT_mark = s_act.n
                    kb = t % 2
                    b.wait(nc.tensor, s_act, xnT_mark)
                    b.wait(nc.tensor, s_w, tbl_w)
                    b.wait(nc.tensor, s_dve, ps_k_rd[kb])
                    b.wait(nc.tensor, s_act, ps_v_rd[kb])
                    for kc in range(NKC):
                        lhs = xnT_t[:, kc * 128:(kc + 1) * 128]
                        nc.tensor.matmul(ps_k[kb][:, :], lhs, wk_sb[:, kc * 512:(kc + 1) * 512],
                                         start=(kc == 0), stop=(kc == NKC - 1))
                        ins = nc.tensor.matmul(ps_v[kb][:, :], lhs, wv_sb[:, kc * 512:(kc + 1) * 512],
                                               start=(kc == 0), stop=(kc == NKC - 1))
                    inc(s_pe, ins)
                    xnT_rd = s_pe.n
                    kv_marks[t] = s_pe.n

                def s_norm(t):
                    xbuf = t % 2
                    kb = t % 2
                    b.wait(nc.scalar, s_pe, kv_marks.pop(t))
                    inc(s_act, nc.scalar.activation(V[:, t * 512:(t + 1) * 512], ps_v[kb][:, :], AF.Copy))
                    ps_v_rd[kb] = s_act.n
                    for g in range(NG):
                        ins = nc.scalar.activation(sq_scr[:, g * 128:(g + 1) * 128],
                                                   ps_k[kb][:, g * 128:(g + 1) * 128],
                                                   AF.Square, accum_out=rstdc[:, 4 + g:5 + g])
                    inc(s_act, ins)
                    b.wait(nc.scalar, s_act)
                    inc(s_act, nc.scalar.activation(rstdc[:, 8:12], rstdc[:, 4:8], AF.Ln,
                                                    bias=C_EPSRMS, scale=1.0 / DH))
                    b.wait(nc.scalar, s_act)
                    inc(s_act, nc.scalar.activation(rstdc[:, 8:12], rstdc[:, 8:12], AF.Exp,
                                                    scale=-0.5, bias=C_LNMULT))
                    b.wait(nc.vector, s_act)
                    for g in range(NG):
                        ins = nc.vector.scalar_tensor_tensor(
                            kn_scr[:, g * 128:(g + 1) * 128], ps_k[kb][:, g * 128:(g + 1) * 128],
                            rstdc[:, 8 + g:9 + g], gkb_sb[:, g * 128:(g + 1) * 128],
                            op0=OP.mult, op1=OP.mult)
                    inc(s_dve, ins)
                    ps_k_rd[kb] = s_dve.n
                    b.wait(nc.vector, s_dve)
                    b.wait(nc.vector, s_kc[xbuf])
                    cos_b = kc_t[xbuf][:, :].unsqueeze(1).broadcast_to([128, NG, 64])
                    sin_b = ks_t[xbuf][:, :].unsqueeze(1).broadcast_to([128, NG, 64])
                    kr_marks[t] = emit_rope(kn_scr[:, :], krope[t % 2][:, :], cos_b, sin_b, NG, scr, ktr_rd[t % 2])
                    kct_rd[xbuf] = s_dve.n

                def s_ktr(t):
                    pb = t % 2
                    b.wait(nc.tensor, s_dve, kr_marks.pop(t))
                    b.wait(nc.tensor, s_act, ps_t_rd[pb])
                    for g in range(NG):
                        ins = nc.tensor.matmul(ps_t[pb][:, g * 128:(g + 1) * 128],
                                               krope[t % 2][:, g * 128:(g + 1) * 128],
                                               ident_sb[:, :], start=True, stop=True)
                    inc(s_pe, ins)
                    ktr_rd[t % 2] = s_pe.n
                    b.wait(nc.scalar, s_pe)
                    dst = KT[:, :].rearrange("p (g n) -> p g n", g=NG)[:, :, t * 128:(t + 1) * 128]
                    srcv = ps_t[pb][:, :].rearrange("p (g n) -> p g n", n=128)
                    inc(s_act, nc.scalar.activation(dst, srcv, AF.Copy))
                    ps_t_rd[pb] = s_act.n

                s_load(0)
                s_front(0)
                for t in range(NT):
                    if t + 1 < NT:
                        s_load(t + 1)
                        emit_conversions(conv_buf, 1)
                        s_front(t + 1)
                    if t >= 1:
                        s_ktr(t - 1)
                    s_norm(t)
                s_ktr(NT - 1)

                p_pe, p_dve, p_act = s_pe.n, s_dve.n, s_act.n
                for eng in (nc.sync, nc.tensor, nc.vector, nc.scalar, nc.gpsimd):
                    b.wait(eng, s_pe, p_pe)
                    b.wait(eng, s_dve, p_dve)
                    b.wait(eng, s_act, p_act)

            with ExitStack() as p1b:
                sb1 = lambda n, s, d: p1b.enter_context(nc.sbuf_tensor(n, s, d))
                ps1 = lambda n, s: p1b.enter_context(nc.psum_tensor(n, s, F32))
                qc_t = [sb1(f"qc_t{i}", [128, 64], F32) for i in range(4)]
                qs_t = [sb1(f"qs_t{i}", [128, 64], F32) for i in range(4)]
                gqb_sb = sb1("gqb_sb", [128, D], F32)
                wq_dc = sb1("wq_dc0", [128, NKC * 512], BF16)
                x_buf = [sb1(f"x1b_{i}", [128, D], F32) for i in range(2)]
                xn_buf = [sb1(f"xn1b{i}", [128, D], BF16) for i in range(2)]
                xnT_m = sb1("xnT_m", [128, NKC * 512], BF16)
                scr = sb1("scr1b", [128, 1024], F32)
                qn_scr = sb1("qn_scr", [128, 512], F32)
                qrope = sb1("qrope", [128, 512], BF16)
                sq_scr = sb1("sq1b", [128, 512], BF16)
                bn_tmp = sb1("bn1b", [128, 24], F32)
                mnv = sb1("mnv1b", [128, 2], F32)
                rstdc = sb1("rstd1b", [128, 8], F32)
                ps_t = [ps1(f"ps1b_t{i}", [128, 512]) for i in range(2)]
                ps_q = [ps1(f"ps1b_q{i}", [128, 512]) for i in range(2)]

                nc.sync.dma_start(gqb_sb[:, :], gqb[:, :]).then_inc(s_ld.h, 16); s_ld.n += 16
                tbl1b = s_ld.n
                for eng in (nc.tensor, nc.vector, nc.scalar, nc.gpsimd, nc.sync):
                    b.wait(eng, s_ld, tbl1b)

                x_rd = [0, 0]
                qct_rd = [0, 0, 0, 0]
                xn_rd_l = [0, 0]
                qtr_rd = 0
                ps_t_rd = [0, 0]
                ps_q_rd = [0, 0]
                wq_rd = 0
                wq_ready = 0
                xnT_rd = 0

                for qm in range(4):
                    for i in range(4):
                        slot = qm * 4 + i
                        xbuf = slot % 2
                        b.wait(nc.sync, s_dve, x_rd[xbuf])
                        nc.sync.dma_start(x_buf[xbuf][:, :],
                                          xq[slot * 128:(slot + 1) * 128, :]).then_inc(s_x[xbuf].h, 16)
                        s_x[xbuf].n += 16
                        b.wait(nc.sync, s_dve, qct_rd[i])
                        nc.sync.dma_start(qc_t[i][:, :], qcos[:, slot * 64:(slot + 1) * 64]).then_inc(s_kc[i % 2].h, 16)
                        nc.sync.dma_start(qs_t[i][:, :], qsin[:, slot * 64:(slot + 1) * 64]).then_inc(s_kc[i % 2].h, 16)
                        s_kc[i % 2].n += 32
                        b.wait(nc.vector, s_x[xbuf])
                        xn_mark = emit_ln(x_buf[xbuf][:, :], xn_buf[xbuf][:, :], bn_tmp, mnv, rstdc, xn_rd_l[xbuf])
                        x_rd[xbuf] = s_dve.n
                        b.wait(nc.tensor, s_dve, xn_mark)
                        for bi in range(4):
                            pb = bi % 2
                            b.wait(nc.tensor, s_act, ps_t_rd[pb])
                            b.wait(nc.tensor, s_pe, xnT_rd)
                            for j in range(4):
                                kc = bi * 4 + j
                                ins = nc.tensor.matmul(ps_t[pb][:, j * 128:(j + 1) * 128],
                                                       xn_buf[xbuf][:, kc * 128:(kc + 1) * 128],
                                                       ident_sb[:, :], start=True, stop=True)
                            inc(s_pe, ins)
                            b.wait(nc.scalar, s_pe)
                            dst = xnT_m[:, :].rearrange("p (kc n) -> p kc n", n=512)[
                                :, bi * 4:(bi + 1) * 4, i * 128:(i + 1) * 128]
                            src = ps_t[pb][:, :].rearrange("p (j n) -> p j n", n=128)
                            inc(s_act, nc.scalar.activation(dst, src, AF.Copy))
                            ps_t_rd[pb] = s_act.n
                        xn_rd_l[xbuf] = s_pe.n
                    xnT_mark = s_act.n
                    for dc in range(4):
                        b.wait(nc.gpsimd, s_pe, wq_rd)
                        nc.gpsimd.dma_start(wq_dc[:, :],
                                            wq_r[:, dc * 8192:(dc + 1) * 8192]).then_inc(s_wq.h, 16)
                        s_wq.n += 16
                        wq_ready = s_wq.n
                        for i in range(4):
                            slot = qm * 4 + i
                            qb = i % 2
                            b.wait(nc.tensor, s_act, xnT_mark)
                            b.wait(nc.tensor, s_wq, wq_ready)
                            b.wait(nc.tensor, s_dve, ps_q_rd[qb])
                            for kc in range(NKC):
                                ins = nc.tensor.matmul(
                                    ps_q[qb][:, :],
                                    xnT_m[:, kc * 512 + i * 128: kc * 512 + (i + 1) * 128],
                                    wq_dc[:, kc * 512:(kc + 1) * 512],
                                    start=(kc == 0), stop=(kc == NKC - 1))
                            inc(s_pe, ins)
                            wq_rd = s_pe.n
                            q_mark = s_pe.n
                            b.wait(nc.scalar, s_pe, q_mark)
                            for hl in range(4):
                                ins = nc.scalar.activation(sq_scr[:, hl * 128:(hl + 1) * 128],
                                                           ps_q[qb][:, hl * 128:(hl + 1) * 128],
                                                           AF.Square, accum_out=rstdc[:, 4 + hl:5 + hl])
                            inc(s_act, ins)
                            b.wait(nc.scalar, s_act)
                            inc(s_act, nc.scalar.activation(rstdc[:, 0:4], rstdc[:, 4:8], AF.Ln,
                                                            bias=C_EPSRMS, scale=1.0 / DH))
                            b.wait(nc.scalar, s_act)
                            inc(s_act, nc.scalar.activation(rstdc[:, 0:4], rstdc[:, 0:4], AF.Exp,
                                                            scale=-0.5, bias=C_LNMULT))
                            b.wait(nc.vector, s_act)
                            for hl in range(4):
                                ins = nc.vector.scalar_tensor_tensor(
                                    qn_scr[:, hl * 128:(hl + 1) * 128],
                                    ps_q[qb][:, hl * 128:(hl + 1) * 128],
                                    rstdc[:, hl:hl + 1],
                                    gqb_sb[:, dc * 512 + hl * 128: dc * 512 + (hl + 1) * 128],
                                    op0=OP.mult, op1=OP.mult)
                            inc(s_dve, ins)
                            ps_q_rd[qb] = s_dve.n
                            b.wait(nc.vector, s_dve)
                            b.wait(nc.vector, s_kc[0])
                            b.wait(nc.vector, s_kc[1])
                            cos_b = qc_t[i][:, :].unsqueeze(1).broadcast_to([128, 4, 64])
                            sin_b = qs_t[i][:, :].unsqueeze(1).broadcast_to([128, 4, 64])
                            qr_mark = emit_rope(qn_scr[:, :], qrope[:, :], cos_b, sin_b, 4, scr, qtr_rd)
                            qct_rd[i] = s_dve.n
                            pb = (dc * 4 + i) % 2
                            b.wait(nc.tensor, s_dve, qr_mark)
                            b.wait(nc.tensor, s_act, ps_t_rd[pb])
                            for hl in range(4):
                                ins = nc.tensor.matmul(ps_t[pb][:, hl * 128:(hl + 1) * 128],
                                                       qrope[:, hl * 128:(hl + 1) * 128],
                                                       ident_sb[:, :], start=True, stop=True)
                            inc(s_pe, ins)
                            qtr_rd = s_pe.n
                            b.wait(nc.scalar, s_pe)
                            dst = QT_v[:, dc * 4:(dc + 1) * 4, slot * 128:(slot + 1) * 128]
                            src = ps_t[pb][:, :].rearrange("p (j n) -> p j n", n=128)
                            inc(s_act, nc.scalar.activation(dst, src, AF.Copy))
                            ps_t_rd[pb] = s_act.n

                p_pe, p_dve, p_act = s_pe.n, s_dve.n, s_act.n
                for eng in (nc.sync, nc.tensor, nc.vector, nc.scalar, nc.gpsimd):
                    b.wait(eng, s_pe, p_pe)
                    b.wait(eng, s_dve, p_dve)
                    b.wait(eng, s_act, p_act)

            if dbg:
                nc.gpsimd.dma_start(dbg_kt[:, :], KT[:, :]).then_inc(s_out.h, 16); s_out.n += 16
                nc.gpsimd.dma_start(dbg_v[:, :], V[:, :]).then_inc(s_out.h, 16); s_out.n += 16
                nc.gpsimd.wait_ge(s_out.h, s_out.n)
                for eng in (nc.sync, nc.tensor, nc.vector, nc.scalar):
                    b.wait(eng, s_out, s_out.n)

            with ExitStack() as p2:
                sb2 = lambda n, s, d: p2.enter_context(nc.sbuf_tensor(n, s, d))
                ps2 = lambda n, s: p2.enter_context(nc.psum_tensor(n, s, F32))
                P_all = [sb2(f"P_all{i}", [128, 4 * NT * 128], BF16) for i in range(2)]
                PT_sb = [sb2(f"PT_sb{i}", [128, 512], BF16) for i in range(2)]
                nm_mac = sb2("nm_mac", [128, 4 * 8], F32)
                sums_mac = sb2("sums_mac", [128, 4 * 8], F32)
                rfac = sb2("rfac", [128, 4 * 8], F32)
                fcol = sb2("fcol", [128, 4 * 8], F32)
                nm_fin = sb2("nm_fin", [128, 4], F32)
                rowsum = sb2("rowsum", [128, 4], F32)
                recip = sb2("recip", [128, 4], F32)
                wsum = sb2("wsum", [128, 4 * 8], F32)
                ps_s = [ps2(f"ps_s{i}", [128, 1024]) for i in range(2)]
                ps_pt = [ps2(f"ps_pt{i}", [128, 512]) for i in range(2)]
                ps_o = [ps2(f"ps_o{i}", [128, 512]) for i in range(2)]

                ps_s_rd = [0, 0]
                ps_pt_rd = [0, 0]
                pt_sb_rd = [0, 0]
                ps_o_rd = [0, 0]
                pall_rd = [0, 0]
                smac = [0]

                resc_marks = {}

                def emit_score_step(u, hl, mp):
                    slot, g = u // 4, u % 4
                    ub = u % 2
                    E = slot_extent(slot)
                    M2 = (E + 7) // 8
                    h = g * 4 + hl
                    w = min(1024, (E - 8 * mp) * 128)
                    sbuf = smac[0] % 2
                    smac[0] += 1
                    b.wait(nc.tensor, s_act, ps_s_rd[sbuf])
                    ins = nc.tensor.matmul(
                        ps_s[sbuf][:, 0:min(512, w)],
                        QT_OT[:, (h * NSLOT + slot) * 128:(h * NSLOT + slot + 1) * 128],
                        KT[:, g * T + mp * 1024: g * T + mp * 1024 + min(512, w)],
                        start=True, stop=True)
                    if w > 512:
                        ins = nc.tensor.matmul(
                            ps_s[sbuf][:, 512:w],
                            QT_OT[:, (h * NSLOT + slot) * 128:(h * NSLOT + slot + 1) * 128],
                            KT[:, g * T + mp * 1024 + 512: g * T + mp * 1024 + w],
                            start=True, stop=True)
                    inc(s_pe, ins)
                    mm_mark = s_pe.n
                    if mp == M2 - 1:
                        stype = slot % 2
                        b.wait(nc.vector, s_pe, mm_mark)
                        o0 = (E - 2 - 8 * mp) * 128
                        o1 = (E - 1 - 8 * mp) * 128
                        nc.vector.tensor_tensor(
                            ps_s[sbuf][:, o0:o0 + 128], ps_s[sbuf][:, o0:o0 + 128],
                            masks_sb[:, (stype * 2) * 128:(stype * 2 + 1) * 128], op=OP.add)
                        ins = nc.vector.tensor_tensor(
                            ps_s[sbuf][:, o1:o1 + 128], ps_s[sbuf][:, o1:o1 + 128],
                            masks_sb[:, (stype * 2 + 1) * 128:(stype * 2 + 2) * 128], op=OP.add)
                        inc(s_dve, ins)
                        b.wait(nc.vector, s_dve)
                    b.wait(nc.vector, s_pe, mm_mark)
                    ins = nc.vector.reduce_max(nm_mac[:, hl * 8 + mp:hl * 8 + mp + 1],
                                               ps_s[sbuf][:, 0:w], axis=AX.X, negate=True)
                    inc(s_dve, ins)
                    b.wait(nc.scalar, s_dve)
                    ins = nc.scalar.activation(
                        P_all[ub][:, hl * NT * 128 + mp * 1024: hl * NT * 128 + mp * 1024 + w],
                        ps_s[sbuf][:, 0:w], AF.Exp,
                        bias=nm_mac[:, hl * 8 + mp:hl * 8 + mp + 1],
                        accum_out=sums_mac[:, hl * 8 + mp:hl * 8 + mp + 1])
                    inc(s_act, ins)
                    ps_s_rd[sbuf] = s_act.n

                def emit_finalize(u):
                    slot, g = u // 4, u % 4
                    ub = u % 2
                    E = slot_extent(slot)
                    M2 = (E + 7) // 8
                    nm3 = nm_mac[:, :].rearrange("p (h m) -> p h m", m=8)[:, :, 0:M2]
                    sm3 = sums_mac[:, :].rearrange("p (h m) -> p h m", m=8)[:, :, 0:M2]
                    rf3 = rfac[:, :].rearrange("p (h m) -> p h m", m=8)[:, :, 0:M2]
                    ws3 = wsum[:, :].rearrange("p (h m) -> p h m", m=8)[:, :, 0:M2]
                    fc3 = fcol[:, :].rearrange("p (h m) -> p h m", m=8)[:, :, 0:M2]
                    b.wait(nc.vector, s_act)
                    inc(s_dve, nc.vector.tensor_reduce(nm_fin[:, 0:4], nm3, axis=AX.X, op=OP.min))
                    b.wait(nc.vector, s_dve)
                    nf_b = nm_fin[:, 0:4].unsqueeze(2).broadcast_to([128, 4, M2])
                    inc(s_dve, nc.vector.tensor_tensor(rf3, nm3, nf_b, op=OP.subtract))
                    b.wait(nc.scalar, s_dve)
                    inc(s_act, nc.scalar.activation(rfac[:, :], rfac[:, :], AF.Exp, scale=-1.0))
                    b.wait(nc.vector, s_act)
                    inc(s_dve, nc.vector.tensor_tensor(ws3, rf3, sm3, op=OP.mult))
                    b.wait(nc.vector, s_dve)
                    inc(s_dve, nc.vector.tensor_reduce(rowsum[:, 0:4], ws3, axis=AX.X, op=OP.add))
                    b.wait(nc.vector, s_dve)
                    inc(s_dve, nc.vector.reciprocal(recip[:, 0:4], rowsum[:, 0:4]))
                    b.wait(nc.vector, s_dve)
                    rc_b = recip[:, 0:4].unsqueeze(2).broadcast_to([128, 4, M2])
                    inc(s_dve, nc.vector.tensor_tensor(fc3, rf3, rc_b, op=OP.mult))
                    b.wait(nc.vector, s_dve)
                    b.wait(nc.scalar, s_dve)
                    for hl in range(4):
                        for mp in range(M2):
                            w = min(1024, (E - 8 * mp) * 128)
                            sl = P_all[ub][:, hl * NT * 128 + mp * 1024: hl * NT * 128 + mp * 1024 + w]
                            if hl < 2:
                                ins = nc.vector.tensor_scalar(
                                    sl, sl, fcol[:, hl * 8 + mp:hl * 8 + mp + 1], None,
                                    op0=OP.mult)
                                inc(s_dve, ins)
                            else:
                                ins = nc.scalar.activation(
                                    sl, sl, AF.Copy, scale=fcol[:, hl * 8 + mp:hl * 8 + mp + 1])
                                inc(s_act, ins)
                    resc_marks[u] = (s_dve.n, s_act.n)

                pt_evac_n = {}

                def emit_pt_step(u, jt):
                    slot, g = u // 4, u % 4
                    ub = u % 2
                    if jt == 0:
                        b.wait(nc.tensor, s_dve, resc_marks[u][0])
                        b.wait(nc.tensor, s_act, resc_marks[u][1])
                    tb = jt % 2
                    b.wait(nc.tensor, s_act, ps_pt_rd[tb])
                    for hl in range(4):
                        ins = nc.tensor.matmul(
                            ps_pt[tb][:, hl * 128:(hl + 1) * 128],
                            P_all[ub][:, hl * NT * 128 + jt * 128: hl * NT * 128 + (jt + 1) * 128],
                            ident_sb[:, :], start=True, stop=True)
                    inc(s_pe, ins)
                    pt_mark = s_pe.n
                    b.wait(nc.scalar, s_pe, pt_mark)
                    b.wait(nc.scalar, s_pe, pt_sb_rd[tb])
                    inc(s_act, nc.scalar.activation(PT_sb[tb][:, :], ps_pt[tb][:, :], AF.Copy))
                    ps_pt_rd[tb] = s_act.n
                    pt_evac_n[(u, jt)] = s_act.n

                def emit_pvmm_step(u, jt):
                    slot, g = u // 4, u % 4
                    ub = u % 2
                    E = slot_extent(slot)
                    tb = jt % 2
                    if jt == 0:
                        b.wait(nc.tensor, s_act, ps_o_rd[ub])
                    b.wait(nc.tensor, s_act, pt_evac_n.pop((u, jt)))
                    ins = nc.tensor.matmul(ps_o[ub][:, :],
                                           V[:, (jt * NG + g) * 128:(jt * NG + g + 1) * 128],
                                           PT_sb[tb][:, :],
                                           start=(jt == 0), stop=(jt == E - 1))
                    inc(s_pe, ins)
                    pt_sb_rd[tb] = s_pe.n
                    if jt == E - 1:
                        pall_rd[ub] = s_pe.n
                        pv_mark = s_pe.n
                        b.wait(nc.scalar, s_pe, pv_mark)
                        dst = QT_v[:, g * 4:(g + 1) * 4, slot * 128:(slot + 1) * 128]
                        src = ps_o[ub][:, :].rearrange("p (hl n) -> p hl n", n=128)
                        inc(s_act, nc.scalar.activation(dst, src, AF.Copy))
                        ps_o_rd[ub] = s_act.n

                def unit_steps(u):
                    slot = u // 4
                    E = slot_extent(slot)
                    M2 = (E + 7) // 8
                    return [(hl, mp) for hl in range(4) for mp in range(M2)], E

                conv_buf2 = [p2.enter_context(nc.sbuf_tensor(f"cvd{i}", [128, CONV_COLS], BF16))
                             for i in range(2)]
                for u in range(64):
                    emit_conversions(conv_buf2, 7)
                    ssteps, _ = unit_steps(u)
                    psteps = unit_steps(u - 1)[1] if u >= 1 else 0
                    ub = u % 2
                    b.wait(nc.scalar, s_pe, pall_rd[ub])
                    n = max(len(ssteps), psteps)
                    pv_done = 0
                    for k in range(n):
                        if k < len(ssteps):
                            emit_score_step(u, *ssteps[k])
                        if k < psteps:
                            emit_pt_step(u - 1, k)
                        if k >= 1 and pv_done < min(k, psteps):
                            emit_pvmm_step(u - 1, pv_done)
                            pv_done += 1
                    while pv_done < psteps:
                        emit_pvmm_step(u - 1, pv_done)
                        pv_done += 1
                    emit_finalize(u)
                for jt in range(unit_steps(63)[1]):
                    emit_pt_step(63, jt)
                    if jt >= 1:
                        emit_pvmm_step(63, jt - 1)
                emit_pvmm_step(63, unit_steps(63)[1] - 1)
                assert conv_state["i"] == len(conv_jobs), conv_state["i"]

                p_pe, p_dve, p_act = s_pe.n, s_dve.n, s_act.n
                for eng in (nc.sync, nc.tensor, nc.vector, nc.scalar, nc.gpsimd):
                    b.wait(eng, s_pe, p_pe)
                    b.wait(eng, s_dve, p_dve)
                    b.wait(eng, s_act, p_act)

        if dbg:
            nc.gpsimd.dma_start(dbg_ot[:, :], QT_OT[:, :]).then_inc(s_out.h, 16); s_out.n += 16
            nc.gpsimd.wait_ge(s_out.h, s_out.n)
            for eng in (nc.sync, nc.tensor, nc.vector, nc.scalar):
                b.wait(eng, s_out, s_out.n)

        with ExitStack() as p25:
            sb5 = lambda n, s, d: p25.enter_context(nc.sbuf_tensor(n, s, d))
            wo_sb = sb5("wo_sb", [128, NKC * 2048], BF16)
            x_buf = [sb5(f"x25_{i}", [128, D], F32) for i in range(2)]
            res_sb = [sb5(f"res_sb{i}", [128, 512], F32) for i in range(2)]
            ps25 = [p25.enter_context(nc.psum_tensor(f"ps25_{i}", [128, 512], F32))
                    for i in range(4)]
            b.wait(nc.sync, s_cw[0])
            b.wait(nc.sync, s_cw[1])
            nc.sync.dma_start(wo_sb[:, :], wo_s[:, :]).then_inc(s_w.h, 16); s_w.n += 16
            wo_ready = s_w.n
            b.wait(nc.tensor, s_w, wo_ready)
            x_rd = [0, 0]
            ps25_rd = [0, 0, 0, 0]
            for slot in range(NSLOT):
                xbuf = slot % 2
                b.wait(nc.sync, s_dve, x_rd[xbuf])
                nc.sync.dma_start(x_buf[xbuf][:, :],
                                  xq[slot * 128:(slot + 1) * 128, :]).then_inc(s_x[xbuf].h, 16)
                s_x[xbuf].n += 16
                for dc in range(4):
                    pq = (slot * 4 + dc) % 4
                    b.wait(nc.tensor, s_dve, ps25_rd[pq])
                    for kc in range(NKC):
                        ins = nc.tensor.matmul(
                            ps25[pq][:, :],
                            QT_OT[:, (kc * NSLOT + slot) * 128:(kc * NSLOT + slot + 1) * 128],
                            wo_sb[:, kc * 2048 + dc * 512: kc * 2048 + (dc + 1) * 512],
                            start=(kc == 0), stop=(kc == NKC - 1))
                    inc(s_pe, ins)
                    wo_mark = s_pe.n
                    rb = (slot * 4 + dc) % 2
                    b.wait(nc.vector, s_pe, wo_mark)
                    b.wait(nc.vector, s_x[xbuf])
                    b.wait(nc.vector, s_r1[rb])
                    ins = nc.vector.tensor_tensor(res_sb[rb][:, :], ps25[pq][:, :],
                                                  x_buf[xbuf][:, dc * 512:(dc + 1) * 512], op=OP.add)
                    inc(s_dve, ins)
                    ps25_rd[pq] = s_dve.n
                    x_rd[xbuf] = s_dve.n
                    b.wait(nc.sync, s_dve, s_dve.n)
                    nc.sync.dma_start(res1[slot * 128:(slot + 1) * 128, dc * 512:(dc + 1) * 512],
                                      res_sb[rb][:, :]).then_inc(s_r1[rb].h, 16)
                    s_r1[rb].n += 16
            p_pe, p_dve, p_act = s_pe.n, s_dve.n, s_act.n
            for eng in (nc.sync, nc.tensor, nc.vector, nc.scalar, nc.gpsimd):
                b.wait(eng, s_pe, p_pe)
                b.wait(eng, s_dve, p_dve)
                b.wait(eng, s_act, p_act)
                b.wait(eng, s_r1[0])
                b.wait(eng, s_r1[1])

    with ExitStack() as p3:
        sb3 = lambda n, s, d: p3.enter_context(nc.sbuf_tensor(n, s, d))
        gated = sb3("gated", [128, 32 * 512], BF16)
        out_acc = sb3("out_acc", [128, 4 * 2048], F32)
        xnT_g = sb3("xnT_g", [128, NKC * 512], BF16)
        x_buf = [sb3(f"x3_{i}", [128, D], F32) for i in range(2)]
        xn_buf = [sb3(f"xn3{i}", [128, D], BF16) for i in range(2)]
        wgu_buf = [sb3(f"wgu{i}", [128, 2 * 2048], BF16) for i in range(4)]
        wd_buf = [sb3(f"wd{i}", [128, 16 * 512], BF16) for i in range(2)]
        r1_buf = [sb3(f"r1b{i}", [128, 4 * 512], F32) for i in range(2)]
        gate_bf = [sb3(f"gate_bf{i}", [128, 512], BF16) for i in range(2)]
        bn_tmp = sb3("bn3", [128, 24], F32)
        mnv = sb3("mnv3", [128, 2], F32)
        rstdc = sb3("rstd3", [128, 8], F32)
        ps_g = [p3.enter_context(nc.psum_tensor(f"ps3g{i}", [128, 512], F32)) for i in range(2)]
        ps_u = [p3.enter_context(nc.psum_tensor(f"ps3u{i}", [128, 512], F32)) for i in range(2)]
        ps_d = [p3.enter_context(nc.psum_tensor(f"ps3d{i}", [128, 512], F32)) for i in range(4)]

        x_rd = [0, 0]
        xn_rd_l = [0, 0]
        xnT_rd = 0
        ps_g_rd = [0, 0]
        ps_u_rd = [0, 0]
        gate_rd = [0, 0]
        wgu_rd = [0, 0, 0, 0]
        wgu_ready = [0, 0, 0, 0]
        wd_rd = [0, 0]
        wd_ready = [0, 0]
        r1_rd = [0, 0]
        ps_d_rd = [0, 0, 0, 0]
        st_done = [0, 0]

        for gi in range(4):
            for i in range(4):
                slot = gi * 4 + i
                xbuf = i % 2
                b.wait(nc.sync, s_dve, x_rd[xbuf])
                nc.sync.dma_start(x_buf[xbuf][:, :],
                                  xq[slot * 128:(slot + 1) * 128, :]).then_inc(s_x[xbuf].h, 16)
                s_x[xbuf].n += 16
                b.wait(nc.vector, s_x[xbuf])
                xn_mark = emit_ln(x_buf[xbuf][:, :], xn_buf[xbuf][:, :], bn_tmp, mnv, rstdc, xn_rd_l[xbuf])
                x_rd[xbuf] = s_dve.n
                b.wait(nc.tensor, s_dve, xn_mark)
                for bi in range(4):
                    pb = bi % 2
                    b.wait(nc.tensor, s_act, ps_g_rd[pb])
                    b.wait(nc.tensor, s_pe, xnT_rd)
                    for j in range(4):
                        kc = bi * 4 + j
                        ins = nc.tensor.matmul(ps_g[pb][:, j * 128:(j + 1) * 128],
                                               xn_buf[xbuf][:, kc * 128:(kc + 1) * 128],
                                               ident_sb[:, :], start=True, stop=True)
                    inc(s_pe, ins)
                    b.wait(nc.scalar, s_pe)
                    dst = xnT_g[:, :].rearrange("p (kc n) -> p kc n", n=512)[
                        :, bi * 4:(bi + 1) * 4, i * 128:(i + 1) * 128]
                    src = ps_g[pb][:, :].rearrange("p (j n) -> p j n", n=128)
                    inc(s_act, nc.scalar.activation(dst, src, AF.Copy))
                    ps_g_rd[pb] = s_act.n
                xn_rd_l[xbuf] = s_pe.n
            xnT_mark = s_act.n

            for half in range(2):
                for sp in range(16):
                    wb = sp % 2
                    hc0 = half * 32 + sp * 2
                    b.wait(nc.sync, s_pe, wgu_rd[wb])
                    nc.sync.dma_start(wgu_buf[wb][:, :],
                                      wg_s[:, hc0 * 2048:(hc0 + 2) * 2048]).then_inc(s_wg[wb].h, 16)
                    s_wg[wb].n += 16
                    wgu_ready[wb] = s_wg[wb].n
                    b.wait(nc.sync, s_pe, wgu_rd[2 + wb])
                    nc.sync.dma_start(wgu_buf[2 + wb][:, :],
                                      wu_s[:, hc0 * 2048:(hc0 + 2) * 2048]).then_inc(s_wu[wb].h, 16)
                    s_wu[wb].n += 16
                    wgu_ready[2 + wb] = s_wu[wb].n
                    for par in range(2):
                        hch = sp * 2 + par
                        pg2 = hch % 2
                        b.wait(nc.tensor, s_act, xnT_mark)
                        b.wait(nc.tensor, s_wg[wb], wgu_ready[wb])
                        b.wait(nc.tensor, s_act, ps_g_rd[pg2])
                        for kc in range(NKC):
                            ins = nc.tensor.matmul(
                                ps_g[pg2][:, :],
                                wgu_buf[wb][:, par * 2048 + kc * 128: par * 2048 + (kc + 1) * 128],
                                xnT_g[:, kc * 512:(kc + 1) * 512],
                                start=(kc == 0), stop=(kc == NKC - 1))
                        inc(s_pe, ins)
                        wgu_rd[wb] = s_pe.n
                        g_mark = s_pe.n
                        b.wait(nc.tensor, s_wu[wb], wgu_ready[2 + wb])
                        b.wait(nc.tensor, s_dve, ps_u_rd[pg2])
                        for kc in range(NKC):
                            ins = nc.tensor.matmul(
                                ps_u[pg2][:, :],
                                wgu_buf[2 + wb][:, par * 2048 + kc * 128: par * 2048 + (kc + 1) * 128],
                                xnT_g[:, kc * 512:(kc + 1) * 512],
                                start=(kc == 0), stop=(kc == NKC - 1))
                        inc(s_pe, ins)
                        wgu_rd[2 + wb] = s_pe.n
                        u_mark = s_pe.n
                        b.wait(nc.scalar, s_pe, g_mark)
                        b.wait(nc.scalar, s_dve, gate_rd[pg2])
                        inc(s_act, nc.scalar.activation(gate_bf[pg2][:, :], ps_g[pg2][:, :], AF.Silu))
                        ps_g_rd[pg2] = s_act.n
                        b.wait(nc.vector, s_act, ps_g_rd[pg2])
                        b.wait(nc.vector, s_pe, u_mark)
                        ins = nc.vector.tensor_tensor(gated[:, hch * 512:(hch + 1) * 512],
                                                      gate_bf[pg2][:, :], ps_u[pg2][:, :], op=OP.mult)
                        inc(s_dve, ins)
                        ps_u_rd[pg2] = s_dve.n
                        gate_rd[pg2] = s_dve.n
                gated_mark = s_dve.n
                if dbg == 2 and gi == 0 and half == 0:
                    b.wait(nc.gpsimd, s_dve, gated_mark)
                    b.wait(nc.gpsimd, s_act, s_act.n)
                    nc.gpsimd.dma_start(dbg_gated[:, :], gated[:, :]).then_inc(s_out.h, 16); s_out.n += 16
                    nc.gpsimd.dma_start(dbg_xnt[:, :], xnT_g[:, :]).then_inc(s_out.h, 16); s_out.n += 16
                    nc.gpsimd.wait_ge(s_out.h, s_out.n)
                    for eng in (nc.sync, nc.tensor, nc.vector, nc.scalar):
                        b.wait(eng, s_out, s_out.n)
                for dc in range(4):
                    if half == 0:
                        rb = dc % 2
                        b.wait(nc.sync, s_dve, r1_rd[rb])
                        r1v = res1.rearrange("(s p) d -> p s d", p=128)
                        nc.sync.dma_start(
                            r1_buf[rb][:, :].rearrange("p (tt n) -> p tt n", n=512),
                            r1v[:, gi * 4:(gi + 1) * 4, dc * 512:(dc + 1) * 512]).then_inc(s_r1[rb].h, 16)
                        s_r1[rb].n += 16
                    for sub in range(2):
                        wb = sub
                        sl0 = ((half * 4 + dc) * 2 + sub) * 8192
                        b.wait(nc.sync, s_pe, wd_rd[wb])
                        nc.sync.dma_start(wd_buf[wb][:, :],
                                          wd_s[:, sl0:sl0 + 8192]).then_inc(s_wd[wb].h, 16)
                        s_wd[wb].n += 16
                        wd_ready[wb] = s_wd[wb].n
                        b.wait(nc.tensor, s_dve, gated_mark)
                        b.wait(nc.tensor, s_wd[wb], wd_ready[wb])
                        for tt in range(4):
                            if sub == 0:
                                b.wait(nc.tensor, s_dve, ps_d_rd[tt])
                            for hcs in range(16):
                                hch = sub * 16 + hcs
                                ins = nc.tensor.matmul(
                                    ps_d[tt][:, :],
                                    gated[:, hch * 512 + tt * 128: hch * 512 + tt * 128 + 128],
                                    wd_buf[wb][:, hcs * 512:(hcs + 1) * 512],
                                    start=(sub == 0 and hcs == 0), stop=(sub == 1 and hcs == 15))
                            inc(s_pe, ins)
                        wd_rd[wb] = s_pe.n
                    down_mark = s_pe.n
                    b.wait(nc.vector, s_pe, down_mark)
                    for tt in range(4):
                        dstc = out_acc[:, tt * 2048 + dc * 512: tt * 2048 + (dc + 1) * 512]
                        if half == 0:
                            rb = dc % 2
                            b.wait(nc.vector, s_r1[rb])
                            b.wait(nc.vector, s_st[tt % 2], st_done[tt % 2])
                            ins = nc.vector.tensor_tensor(
                                dstc, ps_d[tt][:, :],
                                r1_buf[rb][:, tt * 512:(tt + 1) * 512], op=OP.add)
                            r1_rd[rb] = s_dve.n + 1
                        else:
                            ins = nc.vector.tensor_tensor(dstc, dstc, ps_d[tt][:, :], op=OP.add)
                        inc(s_dve, ins)
                        ps_d_rd[tt] = s_dve.n
            acc_mark = s_dve.n
            b.wait(nc.sync, s_dve, acc_mark)
            for tt in range(4):
                slot = gi * 4 + tt
                nc.sync.dma_start(out[slot * 128:(slot + 1) * 128, :],
                                  out_acc[:, tt * 2048:(tt + 1) * 2048]).then_inc(s_st[tt % 2].h, 16)
                s_st[tt % 2].n += 16
            st_done = [s_st[0].n, s_st[1].n]

        nc.sync.wait_ge(s_st[0].h, s_st[0].n)
        nc.sync.wait_ge(s_st[1].h, s_st[1].n)
        if s_out.n:
            nc.sync.wait_ge(s_out.h, s_out.n)

    return b


def _host_tables(p):
    own = own_tiles_for_parity(p)
    d = np.arange(64, dtype=np.float64)
    ang = 10000.0 ** (-d / 64.0)
    rows = np.arange(128, dtype=np.float64)[:, None]
    kcos = np.empty((128, NT * 64), np.float32)
    ksin = np.empty((128, NT * 64), np.float32)
    for t in range(NT):
        rad = (t * 128 + rows) * ang[None, :]
        kcos[:, t * 64:(t + 1) * 64] = np.cos(rad).astype(np.float32)
        ksin[:, t * 64:(t + 1) * 64] = np.sin(rad).astype(np.float32)
    qcos = np.empty((128, NSLOT * 64), np.float32)
    qsin = np.empty((128, NSLOT * 64), np.float32)
    for s in range(NSLOT):
        rad = (own[s] * 128 + rows) * ang[None, :]
        qcos[:, s * 64:(s + 1) * 64] = np.cos(rad).astype(np.float32)
        qsin[:, s * 64:(s + 1) * 64] = np.sin(rad).astype(np.float32)
    tri = np.where(np.arange(128)[:, None] >= np.arange(128)[None, :], 0.0, NEG).astype(np.float32)
    full = np.full((128, 128), NEG, np.float32)
    zero = np.zeros((128, 128), np.float32)
    msets = [tri, full, zero, tri] if p == 0 else [zero, tri, tri, full]
    masks = np.concatenate(msets, axis=1).astype(np.float32)
    return own, kcos, ksin, qcos, qsin, masks


def _relayout_weights(wq, wk, wv, wo, wg, wu, wd):
    wq_r = np.ascontiguousarray(
        wq.reshape(16, 128, 4, 512).transpose(1, 2, 0, 3).reshape(128, 4 * 16 * 512))
    wk_r = np.ascontiguousarray(wk.reshape(16, 128, 512).transpose(1, 0, 2).reshape(128, 16 * 512))
    wv_r = np.ascontiguousarray(wv.reshape(16, 128, 512).transpose(1, 0, 2).reshape(128, 16 * 512))
    wo_r = np.ascontiguousarray(wo.reshape(16, 128, 2048).transpose(1, 0, 2).reshape(128, 16 * 2048))
    wg_r = np.ascontiguousarray(
        wg.reshape(64, 128, 16, 128).transpose(3, 0, 2, 1).reshape(128, 64 * 2048))
    wu_r = np.ascontiguousarray(
        wu.reshape(64, 128, 16, 128).transpose(3, 0, 2, 1).reshape(128, 64 * 2048))
    wd_r = np.ascontiguousarray(
        wd.T.reshape(2, 2, 16, 128, 4, 512).transpose(3, 0, 4, 1, 2, 5).reshape(128, 16 * 8192))
    return wq_r, wk_r, wv_r, wo_r, wg_r, wu_r, wd_r


_BUILDER_CACHE = {}


def _get_builder():
    if "b" not in _BUILDER_CACHE:
        _BUILDER_CACHE["b"] = build_program(dbg=False)
    return _BUILDER_CACHE["b"]


def kernel(L0_attn_x, L0_attn_wq, L0_attn_wk, L0_attn_wv, L0_attn_wo,
           L0_attn_gq, L0_attn_gk, L0_ffn_w_gate, L0_ffn_w_up, L0_ffn_w_down):
    x = np.ascontiguousarray(np.asarray(L0_attn_x, dtype=np.float32))
    wq_np = np.asarray(L0_attn_wq, np.float32)
    wk_np = np.asarray(L0_attn_wk, np.float32)
    wv_np = np.asarray(L0_attn_wv, np.float32)
    wo_np = np.asarray(L0_attn_wo, np.float32)
    gq_np = np.asarray(L0_attn_gq, np.float32).reshape(1, D)
    gk_np = np.asarray(L0_attn_gk, np.float32).reshape(1, 512)
    wg_np = np.asarray(L0_ffn_w_gate, np.float32)
    wu_np = np.asarray(L0_ffn_w_up, np.float32)
    wd_np = np.asarray(L0_ffn_w_down, np.float32)
    wq_r, wk_r, wv_r, wo_r, wg_r, wu_r, wd_r = _relayout_weights(
        wq_np, wk_np, wv_np, wo_np, wg_np, wu_np, wd_np)
    gqb_np = np.ascontiguousarray(np.broadcast_to(gq_np, (128, D)))
    gkb_np = np.ascontiguousarray(np.broadcast_to(gk_np, (128, 512)))
    ident_np = np.eye(128, dtype=ml_dtypes.bfloat16)
    consts_np = np.ascontiguousarray(np.broadcast_to(
        np.array([[EPS_LN, EPS_RMS, LNMULT, 0.0]], np.float32), (128, 4)))

    bb = _get_builder()
    in_maps = []
    owns = []
    for c in range(8):
        bidx, p = c // 2, c % 2
        own, kcos_np, ksin_np, qcos_np, qsin_np, masks_np = _host_tables(p)
        owns.append(own)
        xb_np = np.ascontiguousarray(x[bidx])
        xq_np = np.ascontiguousarray(
            xb_np.reshape(NT, 128, D)[own].reshape(NSLOT * 128, D))
        in_maps.append({
            "xb": xb_np, "xq": xq_np,
            "wq_r": wq_r, "wk_r": wk_r, "wv_r": wv_r, "wo_r": wo_r,
            "wg_r": wg_r, "wu_r": wu_r, "wd_r": wd_r,
            "gqb": gqb_np, "gkb": gkb_np,
            "kcos": kcos_np, "ksin": ksin_np,
            "qcos": qcos_np, "qsin": qsin_np,
            "ident": ident_np, "masks": masks_np, "consts": consts_np,
        })

    res = run_bass_kernel_spmd(bb.nc, in_maps, core_ids=list(range(8)))
    outp = np.empty((B, T, D), np.float32)
    for c in range(8):
        bidx = c // 2
        o = np.asarray(res.results[c]["out"]).reshape(NSLOT, 128, D)
        for s, t in enumerate(owns[c]):
            outp[bidx, t * 128:(t + 1) * 128, :] = o[s]
    return outp


# version 14
# speedup vs baseline: 1.0765x; 1.0765x over previous
import numpy as np
import ml_dtypes
from contextlib import ExitStack

import concourse.bass as bass
import concourse.mybir as mybir
from concourse.alu_op_type import AluOpType as OP
from concourse.bass_utils import run_bass_kernel_spmd

F32 = mybir.dt.float32
BF16 = mybir.dt.bfloat16
AF = mybir.ActivationFunctionType
AX = mybir.AxisListType

B, T, D = 4, 4096, 2048
DH, NG, NH = 128, 4, 16
HID = 8192
NT = T // 128
NSLOT = 16
NKC = D // 128
NHC = HID // 128
EPS_LN, EPS_RMS = 1e-5, 1e-6
LNMULT = float(-0.25 * np.log(128.0))
NEG = -1e30
CONV_COLS = 1024


def own_tiles_for_parity(p):
    tiles = []
    for k in range(8):
        tiles += [4 * k, 4 * k + 3] if p == 0 else [4 * k + 1, 4 * k + 2]
    return tiles


def slot_extent(s):
    k = s // 2
    return 4 * k + 2 if s % 2 == 0 else 4 * k + 4


class Sem:
    def __init__(self, stack, nc, name):
        self.h = stack.enter_context(nc.semaphore(name))
        self.n = 0
        self.name = name


class Builder:
    def __init__(self):
        self.nc = bass.Bass("TRN2", target_bir_lowering=False)
        self.stack = ExitStack()
        self.lastw = {}

    def sem(self, name):
        return Sem(self.stack, self.nc, name)

    def sb(self, name, shape, dt):
        return self.stack.enter_context(self.nc.sbuf_tensor(name, shape, dt))

    def wait(self, eng, sem, n=None):
        n = sem.n if n is None else n
        if n <= 0:
            return
        key = (id(eng), sem.name)
        if self.lastw.get(key, -1) >= n:
            return
        eng.wait_ge(sem.h, n)
        self.lastw[key] = n


def build_program(dbg=False):
    b = Builder()
    nc = b.nc

    xb = nc.dram_tensor("xb", [T, D], F32, kind="ExternalInput")
    xq = nc.dram_tensor("xq", [NSLOT * 128, D], F32, kind="ExternalInput")
    wq_r = nc.dram_tensor("wq_r", [128, 4 * NKC * 512], F32, kind="ExternalInput")
    wk_r = nc.dram_tensor("wk_r", [128, NKC * 512], F32, kind="ExternalInput")
    wv_r = nc.dram_tensor("wv_r", [128, NKC * 512], F32, kind="ExternalInput")
    wo_r = nc.dram_tensor("wo_r", [128, NKC * 2048], F32, kind="ExternalInput")
    wg_r = nc.dram_tensor("wg_r", [128, NHC * 2048], F32, kind="ExternalInput")
    wu_r = nc.dram_tensor("wu_r", [128, NHC * 2048], F32, kind="ExternalInput")
    wd_r = nc.dram_tensor("wd_r", [128, 16 * 8192], F32, kind="ExternalInput")
    gqb = nc.dram_tensor("gqb", [128, D], F32, kind="ExternalInput")
    gkb = nc.dram_tensor("gkb", [128, 512], F32, kind="ExternalInput")
    kcos = nc.dram_tensor("kcos", [128, NT * 64], F32, kind="ExternalInput")
    ksin = nc.dram_tensor("ksin", [128, NT * 64], F32, kind="ExternalInput")
    qcos = nc.dram_tensor("qcos", [128, NSLOT * 64], F32, kind="ExternalInput")
    qsin = nc.dram_tensor("qsin", [128, NSLOT * 64], F32, kind="ExternalInput")
    ident = nc.dram_tensor("ident", [128, 128], BF16, kind="ExternalInput")
    masks = nc.dram_tensor("masks", [128, 4 * 128], F32, kind="ExternalInput")
    consts = nc.dram_tensor("consts", [128, 4], F32, kind="ExternalInput")
    out = nc.dram_tensor("out", [NSLOT * 128, D], F32, kind="ExternalOutput")
    wg_s = nc.dram_tensor("wg_s", [128, NHC * 2048], BF16, kind="Internal")
    wu_s = nc.dram_tensor("wu_s", [128, NHC * 2048], BF16, kind="Internal")
    wd_s = nc.dram_tensor("wd_s", [128, 16 * 8192], BF16, kind="Internal")
    wo_s = nc.dram_tensor("wo_s", [128, NKC * 2048], BF16, kind="Internal")
    res1 = nc.dram_tensor("res1", [NSLOT * 128, D], F32,
                          kind=("ExternalOutput" if dbg == 2 else "Internal"))
    if dbg == 2:
        dbg_gated = nc.dram_tensor("dbg_gated", [128, 32 * 512], F32, kind="ExternalOutput")
        dbg_xnt = nc.dram_tensor("dbg_xnt", [128, NKC * 512], F32, kind="ExternalOutput")
    if dbg:
        dbg_kt = nc.dram_tensor("dbg_kt", [128, NG * T], F32, kind="ExternalOutput")
        dbg_v = nc.dram_tensor("dbg_v", [128, NT * NG * 128], F32, kind="ExternalOutput")
        dbg_ot = nc.dram_tensor("dbg_ot", [128, NH * NSLOT * 128], F32, kind="ExternalOutput")

    s_ld = b.sem("s_ld")
    s_w = b.sem("s_w")
    s_out = b.sem("s_out")
    s_x = [b.sem("s_x0"), b.sem("s_x1")]
    s_kc = [b.sem("s_kc0"), b.sem("s_kc1")]
    s_wq = b.sem("s_wq")
    s_wg = [b.sem("s_wg0"), b.sem("s_wg1")]
    s_wu = [b.sem("s_wu0"), b.sem("s_wu1")]
    s_wd = [b.sem("s_wd0"), b.sem("s_wd1")]
    s_r1 = [b.sem("s_r10"), b.sem("s_r11")]
    s_st = [b.sem("s_st0"), b.sem("s_st1")]
    s_cr = [b.sem("s_cr0"), b.sem("s_cr1")]
    s_cw = [b.sem("s_cw0"), b.sem("s_cw1")]
    s_pe = b.sem("s_pe")
    s_dve = b.sem("s_dve")
    s_act = b.sem("s_act")

    def inc(sem, ins, k=1):
        ins.then_inc(sem.h, k)
        sem.n += k

    ident_sb = b.sb("ident_sb", [128, 128], BF16)
    masks_sb = b.sb("masks_sb", [128, 4 * 128], F32)
    consts_sb = b.sb("consts_sb", [128, 4], F32)
    nc.sync.dma_start(ident_sb[:, :], ident[:, :]).then_inc(s_ld.h, 16); s_ld.n += 16
    nc.sync.dma_start(masks_sb[:, :], masks[:, :]).then_inc(s_ld.h, 16); s_ld.n += 16
    nc.sync.dma_start(consts_sb[:, :], consts[:, :]).then_inc(s_ld.h, 16); s_ld.n += 16
    C_EPSLN = consts_sb[:, 0:1]
    C_EPSRMS = consts_sb[:, 1:2]
    C_LNMULT = consts_sb[:, 2:3]

    conv_jobs = []
    for srcw, dstw, ncols in ((wg_r, wg_s, NHC * 2048), (wu_r, wu_s, NHC * 2048),
                              (wd_r, wd_s, 16 * 8192), (wo_r, wo_s, NKC * 2048)):
        for c0 in range(0, ncols, CONV_COLS):
            conv_jobs.append((srcw, dstw, c0))
    conv_state = {"i": 0}

    def emit_conversions(conv_buf, k):
        for _ in range(k):
            i = conv_state["i"]
            if i >= len(conv_jobs):
                return
            conv_state["i"] += 1
            srcw, dstw, c0 = conv_jobs[i]
            par = i % 2
            b.wait(nc.gpsimd, s_cw[par])
            nc.gpsimd.dma_start(conv_buf[par][:, :],
                                srcw[:, c0:c0 + CONV_COLS]).then_inc(s_cr[par].h, 16)
            s_cr[par].n += 16
            b.wait(nc.sync, s_cr[par])
            nc.sync.dma_start(dstw[:, c0:c0 + CONV_COLS],
                              conv_buf[par][:, :]).then_inc(s_cw[par].h, 16)
            s_cw[par].n += 16

    def emit_ln(x_ap, xn_ap, bn_tmp, mnv, rstdc, xn_rd):
        for c4 in range(4):
            ins = nc.vector.bn_stats(bn_tmp[:, c4 * 6:(c4 + 1) * 6],
                                     x_ap[:, c4 * 512:(c4 + 1) * 512])
        inc(s_dve, ins)
        b.wait(nc.vector, s_dve)
        inc(s_dve, nc.vector.bn_aggr(mnv[:, :], bn_tmp[:, :]))
        b.wait(nc.scalar, s_dve)
        inc(s_act, nc.scalar.activation(rstdc[:, 0:1], mnv[:, 1:2], AF.Ln, bias=C_EPSLN))
        b.wait(nc.scalar, s_act)
        inc(s_act, nc.scalar.activation(rstdc[:, 1:2], rstdc[:, 0:1], AF.Exp, scale=-0.5))
        b.wait(nc.vector, s_act)
        b.wait(nc.vector, s_pe, xn_rd)
        ins = nc.vector.tensor_scalar(xn_ap, x_ap, mnv[:, 0:1], rstdc[:, 1:2],
                                      op0=OP.subtract, op1=OP.mult)
        inc(s_dve, ins)
        return s_dve.n

    def emit_rope(src_f32, dst_bf16, cosb, sinb, ngrp, scr, tr_rd):
        sv = src_f32.rearrange("p (g d) -> p g d", d=128)
        dv = dst_bf16.rearrange("p (g d) -> p g d", d=128)
        sA = scr[:, 0:256].rearrange("p (g d) -> p g d", d=64)[:, 0:ngrp, :]
        sB = scr[:, 256:512].rearrange("p (g d) -> p g d", d=64)[:, 0:ngrp, :]
        sC = scr[:, 512:768].rearrange("p (g d) -> p g d", d=64)[:, 0:ngrp, :]
        sD = scr[:, 768:1024].rearrange("p (g d) -> p g d", d=64)[:, 0:ngrp, :]
        ev = sv[:, :, 0:64]
        od = sv[:, :, 64:128]
        nc.vector.tensor_tensor(sA, ev, cosb, op=OP.mult)
        nc.vector.tensor_tensor(sB, od, sinb, op=OP.mult)
        nc.vector.tensor_tensor(sC, ev, sinb, op=OP.mult)
        ins = nc.vector.tensor_tensor(sD, od, cosb, op=OP.mult)
        inc(s_dve, ins)
        b.wait(nc.vector, s_dve)
        b.wait(nc.vector, s_pe, tr_rd)
        nc.vector.tensor_tensor(dv[:, :, 0:64], sA, sB, op=OP.subtract)
        ins = nc.vector.tensor_tensor(dv[:, :, 64:128], sC, sD, op=OP.add)
        inc(s_dve, ins)
        return s_dve.n

    with ExitStack() as S1:
        QT_OT = S1.enter_context(nc.sbuf_tensor("QT_OT", [128, NH * NSLOT * 128], BF16))
        QT_v = QT_OT[:, :].rearrange("p (h n) -> p h n", h=NH)
        with ExitStack() as S2:
            KT = S2.enter_context(nc.sbuf_tensor("KT", [128, NG * T], BF16))
            V = S2.enter_context(nc.sbuf_tensor("V", [128, NT * NG * 128], BF16))

            with ExitStack() as p1:
                sb1 = lambda n, s, d: p1.enter_context(nc.sbuf_tensor(n, s, d))
                ps1 = lambda n, s: p1.enter_context(nc.psum_tensor(n, s, F32))
                kc_t = [sb1(f"kc_t{i}", [128, 64], F32) for i in range(2)]
                ks_t = [sb1(f"ks_t{i}", [128, 64], F32) for i in range(2)]
                gkb_sb = sb1("gkb_sb", [128, 512], F32)
                wk_sb = sb1("wk_sb", [128, NKC * 512], BF16)
                wv_sb = sb1("wv_sb", [128, NKC * 512], BF16)
                x_buf = [sb1(f"x_buf{i}", [128, D], F32) for i in range(2)]
                xn_buf = [sb1(f"xn_buf{i}", [128, D], BF16) for i in range(2)]
                xnT_t = sb1("xnT_t", [128, NKC * 128], BF16)
                scr = sb1("scr", [128, 1024], F32)
                kn_scr = sb1("kn_scr", [128, 512], F32)
                krope = [sb1(f"krope{i}", [128, 512], BF16) for i in range(2)]
                sq_scr = sb1("sq_scr", [128, 512], BF16)
                bn_tmp = sb1("bn_tmp", [128, 24], F32)
                mnv = sb1("mnv", [128, 2], F32)
                rstdc = sb1("rstdc", [128, 12], F32)
                conv_buf = [sb1(f"cvb{i}", [128, CONV_COLS], BF16) for i in range(2)]
                ps_t = [ps1(f"ps_t{i}", [128, 512]) for i in range(2)]
                ps_k = [ps1(f"ps_k{i}", [128, 512]) for i in range(2)]
                ps_v = [ps1(f"ps_v{i}", [128, 512]) for i in range(2)]

                nc.sync.dma_start(gkb_sb[:, :], gkb[:, :]).then_inc(s_ld.h, 16); s_ld.n += 16
                nc.gpsimd.dma_start(wk_sb[:, :], wk_r[:, :]).then_inc(s_w.h, 16); s_w.n += 16
                nc.gpsimd.dma_start(wv_sb[:, :], wv_r[:, :]).then_inc(s_w.h, 16); s_w.n += 16
                tbl_ld, tbl_w = s_ld.n, s_w.n
                for eng in (nc.tensor, nc.vector, nc.scalar, nc.gpsimd, nc.sync):
                    b.wait(eng, s_ld, tbl_ld)

                x_rd = [0, 0]
                kct_rd = [0, 0]
                xn_rd_l = [0, 0]
                ktr_rd = [0, 0]
                ps_t_rd = [0, 0]
                ps_k_rd = [0, 0]
                ps_v_rd = [0, 0]
                xnT_rd = 0
                kv_marks = {}
                kr_marks = {}

                def s_load(t):
                    xbuf = t % 2
                    b.wait(nc.sync, s_dve, x_rd[xbuf])
                    nc.sync.dma_start(x_buf[xbuf][:, :],
                                      xb[t * 128:(t + 1) * 128, :]).then_inc(s_x[xbuf].h, 16)
                    s_x[xbuf].n += 16
                    b.wait(nc.sync, s_dve, kct_rd[xbuf])
                    nc.sync.dma_start(kc_t[xbuf][:, :], kcos[:, t * 64:(t + 1) * 64]).then_inc(s_kc[xbuf].h, 16)
                    nc.sync.dma_start(ks_t[xbuf][:, :], ksin[:, t * 64:(t + 1) * 64]).then_inc(s_kc[xbuf].h, 16)
                    s_kc[xbuf].n += 32

                def s_front(t):
                    nonlocal xnT_rd
                    xbuf = t % 2
                    b.wait(nc.vector, s_x[xbuf])
                    xn_mark = emit_ln(x_buf[xbuf][:, :], xn_buf[xbuf][:, :], bn_tmp, mnv, rstdc, xn_rd_l[xbuf])
                    x_rd[xbuf] = s_dve.n
                    b.wait(nc.tensor, s_dve, xn_mark)
                    for bi in range(4):
                        pb = bi % 2
                        b.wait(nc.tensor, s_act, ps_t_rd[pb])
                        b.wait(nc.tensor, s_pe, xnT_rd)
                        for j in range(4):
                            kc = bi * 4 + j
                            ins = nc.tensor.matmul(ps_t[pb][:, j * 128:(j + 1) * 128],
                                                   xn_buf[xbuf][:, kc * 128:(kc + 1) * 128],
                                                   ident_sb[:, :], start=True, stop=True)
                        inc(s_pe, ins)
                        b.wait(nc.scalar, s_pe)
                        dst = xnT_t[:, :].rearrange("p (kc n) -> p kc n", n=128)[:, bi * 4:(bi + 1) * 4, :]
                        srcv = ps_t[pb][:, :].rearrange("p (j n) -> p j n", n=128)
                        inc(s_act, nc.scalar.activation(dst, srcv, AF.Copy))
                        ps_t_rd[pb] = s_act.n
                    xn_rd_l[xbuf] = s_pe.n
                    xnT_mark = s_act.n
                    kb = t % 2
                    b.wait(nc.tensor, s_act, xnT_mark)
                    b.wait(nc.tensor, s_w, tbl_w)
                    b.wait(nc.tensor, s_dve, ps_k_rd[kb])
                    b.wait(nc.tensor, s_act, ps_v_rd[kb])
                    for kc in range(NKC):
                        lhs = xnT_t[:, kc * 128:(kc + 1) * 128]
                        nc.tensor.matmul(ps_k[kb][:, :], lhs, wk_sb[:, kc * 512:(kc + 1) * 512],
                                         start=(kc == 0), stop=(kc == NKC - 1))
                        ins = nc.tensor.matmul(ps_v[kb][:, :], lhs, wv_sb[:, kc * 512:(kc + 1) * 512],
                                               start=(kc == 0), stop=(kc == NKC - 1))
                    inc(s_pe, ins)
                    xnT_rd = s_pe.n
                    kv_marks[t] = s_pe.n

                def s_norm(t):
                    xbuf = t % 2
                    kb = t % 2
                    b.wait(nc.scalar, s_pe, kv_marks.pop(t))
                    inc(s_act, nc.scalar.activation(V[:, t * 512:(t + 1) * 512], ps_v[kb][:, :], AF.Copy))
                    ps_v_rd[kb] = s_act.n
                    for g in range(NG):
                        ins = nc.scalar.activation(sq_scr[:, g * 128:(g + 1) * 128],
                                                   ps_k[kb][:, g * 128:(g + 1) * 128],
                                                   AF.Square, accum_out=rstdc[:, 4 + g:5 + g])
                    inc(s_act, ins)
                    b.wait(nc.scalar, s_act)
                    inc(s_act, nc.scalar.activation(rstdc[:, 8:12], rstdc[:, 4:8], AF.Ln,
                                                    bias=C_EPSRMS, scale=1.0 / DH))
                    b.wait(nc.scalar, s_act)
                    inc(s_act, nc.scalar.activation(rstdc[:, 8:12], rstdc[:, 8:12], AF.Exp,
                                                    scale=-0.5, bias=C_LNMULT))
                    b.wait(nc.vector, s_act)
                    for g in range(NG):
                        ins = nc.vector.scalar_tensor_tensor(
                            kn_scr[:, g * 128:(g + 1) * 128], ps_k[kb][:, g * 128:(g + 1) * 128],
                            rstdc[:, 8 + g:9 + g], gkb_sb[:, g * 128:(g + 1) * 128],
                            op0=OP.mult, op1=OP.mult)
                    inc(s_dve, ins)
                    ps_k_rd[kb] = s_dve.n
                    b.wait(nc.vector, s_dve)
                    b.wait(nc.vector, s_kc[xbuf])
                    cos_b = kc_t[xbuf][:, :].unsqueeze(1).broadcast_to([128, NG, 64])
                    sin_b = ks_t[xbuf][:, :].unsqueeze(1).broadcast_to([128, NG, 64])
                    kr_marks[t] = emit_rope(kn_scr[:, :], krope[t % 2][:, :], cos_b, sin_b, NG, scr, ktr_rd[t % 2])
                    kct_rd[xbuf] = s_dve.n

                def s_ktr(t):
                    pb = t % 2
                    b.wait(nc.tensor, s_dve, kr_marks.pop(t))
                    b.wait(nc.tensor, s_act, ps_t_rd[pb])
                    for g in range(NG):
                        ins = nc.tensor.matmul(ps_t[pb][:, g * 128:(g + 1) * 128],
                                               krope[t % 2][:, g * 128:(g + 1) * 128],
                                               ident_sb[:, :], start=True, stop=True)
                    inc(s_pe, ins)
                    ktr_rd[t % 2] = s_pe.n
                    b.wait(nc.scalar, s_pe)
                    dst = KT[:, :].rearrange("p (g n) -> p g n", g=NG)[:, :, t * 128:(t + 1) * 128]
                    srcv = ps_t[pb][:, :].rearrange("p (g n) -> p g n", n=128)
                    inc(s_act, nc.scalar.activation(dst, srcv, AF.Copy))
                    ps_t_rd[pb] = s_act.n

                s_load(0)
                s_front(0)
                for t in range(NT):
                    if t + 1 < NT:
                        s_load(t + 1)
                        emit_conversions(conv_buf, 1)
                        s_front(t + 1)
                    if t >= 1:
                        s_ktr(t - 1)
                    s_norm(t)
                s_ktr(NT - 1)

                p_pe, p_dve, p_act = s_pe.n, s_dve.n, s_act.n
                for eng in (nc.sync, nc.tensor, nc.vector, nc.scalar, nc.gpsimd):
                    b.wait(eng, s_pe, p_pe)
                    b.wait(eng, s_dve, p_dve)
                    b.wait(eng, s_act, p_act)

            with ExitStack() as p1b:
                sb1 = lambda n, s, d: p1b.enter_context(nc.sbuf_tensor(n, s, d))
                ps1 = lambda n, s: p1b.enter_context(nc.psum_tensor(n, s, F32))
                qc_t = [sb1(f"qc_t{i}", [128, 64], F32) for i in range(4)]
                qs_t = [sb1(f"qs_t{i}", [128, 64], F32) for i in range(4)]
                gqb_sb = sb1("gqb_sb", [128, D], F32)
                wq_dc = sb1("wq_dc0", [128, NKC * 512], BF16)
                x_buf = [sb1(f"x1b_{i}", [128, D], F32) for i in range(2)]
                xn_buf = [sb1(f"xn1b{i}", [128, D], BF16) for i in range(2)]
                xnT_m = sb1("xnT_m", [128, NKC * 512], BF16)
                scr = sb1("scr1b", [128, 1024], F32)
                qn_scr = sb1("qn_scr", [128, 512], F32)
                qrope = [sb1(f"qrope{i}", [128, 512], BF16) for i in range(2)]
                sq_scr = sb1("sq1b", [128, 512], BF16)
                bn_tmp = sb1("bn1b", [128, 24], F32)
                mnv = sb1("mnv1b", [128, 2], F32)
                rstdc = sb1("rstd1b", [128, 8], F32)
                ps_t = [ps1(f"ps1b_t{i}", [128, 512]) for i in range(2)]
                ps_q = [ps1(f"ps1b_q{i}", [128, 512]) for i in range(2)]

                nc.sync.dma_start(gqb_sb[:, :], gqb[:, :]).then_inc(s_ld.h, 16); s_ld.n += 16
                tbl1b = s_ld.n
                for eng in (nc.tensor, nc.vector, nc.scalar, nc.gpsimd, nc.sync):
                    b.wait(eng, s_ld, tbl1b)

                x_rd = [0, 0]
                qct_rd = [0, 0, 0, 0]
                xn_rd_l = [0, 0]
                qtr_rd = [0, 0]
                ps_t_rd = [0, 0]
                ps_q_rd = [0, 0]
                wq_rd = 0
                wq_ready = 0
                xnT_rd = 0
                qstep = [0]
                pend_qtr = [None]
                qr_marks = {}

                def flush_qtr():
                    if pend_qtr[0] is None:
                        return
                    dcp, slotp, qb2 = pend_qtr[0]
                    pend_qtr[0] = None
                    pb = qb2
                    b.wait(nc.tensor, s_dve, qr_marks.pop((dcp, slotp)))
                    b.wait(nc.tensor, s_act, ps_t_rd[pb])
                    for hl in range(4):
                        ins = nc.tensor.matmul(ps_t[pb][:, hl * 128:(hl + 1) * 128],
                                               qrope[qb2][:, hl * 128:(hl + 1) * 128],
                                               ident_sb[:, :], start=True, stop=True)
                    inc(s_pe, ins)
                    qtr_rd[qb2] = s_pe.n
                    b.wait(nc.scalar, s_pe)
                    dst = QT_v[:, dcp * 4:(dcp + 1) * 4, slotp * 128:(slotp + 1) * 128]
                    srcv = ps_t[pb][:, :].rearrange("p (j n) -> p j n", n=128)
                    inc(s_act, nc.scalar.activation(dst, srcv, AF.Copy))
                    ps_t_rd[pb] = s_act.n

                for qm in range(4):
                    for i in range(4):
                        slot = qm * 4 + i
                        xbuf = slot % 2
                        b.wait(nc.sync, s_dve, x_rd[xbuf])
                        nc.sync.dma_start(x_buf[xbuf][:, :],
                                          xq[slot * 128:(slot + 1) * 128, :]).then_inc(s_x[xbuf].h, 16)
                        s_x[xbuf].n += 16
                        b.wait(nc.sync, s_dve, qct_rd[i])
                        nc.sync.dma_start(qc_t[i][:, :], qcos[:, slot * 64:(slot + 1) * 64]).then_inc(s_kc[i % 2].h, 16)
                        nc.sync.dma_start(qs_t[i][:, :], qsin[:, slot * 64:(slot + 1) * 64]).then_inc(s_kc[i % 2].h, 16)
                        s_kc[i % 2].n += 32
                        b.wait(nc.vector, s_x[xbuf])
                        xn_mark = emit_ln(x_buf[xbuf][:, :], xn_buf[xbuf][:, :], bn_tmp, mnv, rstdc, xn_rd_l[xbuf])
                        x_rd[xbuf] = s_dve.n
                        b.wait(nc.tensor, s_dve, xn_mark)
                        for bi in range(4):
                            pb = bi % 2
                            b.wait(nc.tensor, s_act, ps_t_rd[pb])
                            b.wait(nc.tensor, s_pe, xnT_rd)
                            for j in range(4):
                                kc = bi * 4 + j
                                ins = nc.tensor.matmul(ps_t[pb][:, j * 128:(j + 1) * 128],
                                                       xn_buf[xbuf][:, kc * 128:(kc + 1) * 128],
                                                       ident_sb[:, :], start=True, stop=True)
                            inc(s_pe, ins)
                            b.wait(nc.scalar, s_pe)
                            dst = xnT_m[:, :].rearrange("p (kc n) -> p kc n", n=512)[
                                :, bi * 4:(bi + 1) * 4, i * 128:(i + 1) * 128]
                            src = ps_t[pb][:, :].rearrange("p (j n) -> p j n", n=128)
                            inc(s_act, nc.scalar.activation(dst, src, AF.Copy))
                            ps_t_rd[pb] = s_act.n
                        xn_rd_l[xbuf] = s_pe.n
                    xnT_mark = s_act.n
                    for dc in range(4):
                        b.wait(nc.gpsimd, s_pe, wq_rd)
                        nc.gpsimd.dma_start(wq_dc[:, :],
                                            wq_r[:, dc * 8192:(dc + 1) * 8192]).then_inc(s_wq.h, 16)
                        s_wq.n += 16
                        wq_ready = s_wq.n
                        for i in range(4):
                            slot = qm * 4 + i
                            qb = i % 2
                            b.wait(nc.tensor, s_act, xnT_mark)
                            b.wait(nc.tensor, s_wq, wq_ready)
                            b.wait(nc.tensor, s_dve, ps_q_rd[qb])
                            for kc in range(NKC):
                                ins = nc.tensor.matmul(
                                    ps_q[qb][:, :],
                                    xnT_m[:, kc * 512 + i * 128: kc * 512 + (i + 1) * 128],
                                    wq_dc[:, kc * 512:(kc + 1) * 512],
                                    start=(kc == 0), stop=(kc == NKC - 1))
                            inc(s_pe, ins)
                            wq_rd = s_pe.n
                            q_mark = s_pe.n
                            b.wait(nc.scalar, s_pe, q_mark)
                            for hl in range(4):
                                ins = nc.scalar.activation(sq_scr[:, hl * 128:(hl + 1) * 128],
                                                           ps_q[qb][:, hl * 128:(hl + 1) * 128],
                                                           AF.Square, accum_out=rstdc[:, 4 + hl:5 + hl])
                            inc(s_act, ins)
                            b.wait(nc.scalar, s_act)
                            inc(s_act, nc.scalar.activation(rstdc[:, 0:4], rstdc[:, 4:8], AF.Ln,
                                                            bias=C_EPSRMS, scale=1.0 / DH))
                            b.wait(nc.scalar, s_act)
                            inc(s_act, nc.scalar.activation(rstdc[:, 0:4], rstdc[:, 0:4], AF.Exp,
                                                            scale=-0.5, bias=C_LNMULT))
                            b.wait(nc.vector, s_act)
                            for hl in range(4):
                                ins = nc.vector.scalar_tensor_tensor(
                                    qn_scr[:, hl * 128:(hl + 1) * 128],
                                    ps_q[qb][:, hl * 128:(hl + 1) * 128],
                                    rstdc[:, hl:hl + 1],
                                    gqb_sb[:, dc * 512 + hl * 128: dc * 512 + (hl + 1) * 128],
                                    op0=OP.mult, op1=OP.mult)
                            inc(s_dve, ins)
                            ps_q_rd[qb] = s_dve.n
                            b.wait(nc.vector, s_dve)
                            b.wait(nc.vector, s_kc[0])
                            b.wait(nc.vector, s_kc[1])
                            cos_b = qc_t[i][:, :].unsqueeze(1).broadcast_to([128, 4, 64])
                            sin_b = qs_t[i][:, :].unsqueeze(1).broadcast_to([128, 4, 64])
                            qb2 = qstep[0] % 2
                            qstep[0] += 1
                            qr_marks[(dc, slot)] = emit_rope(qn_scr[:, :], qrope[qb2][:, :],
                                                            cos_b, sin_b, 4, scr, qtr_rd[qb2])
                            qct_rd[i] = s_dve.n
                            flush_qtr()
                            pend_qtr[0] = (dc, slot, qb2)

                flush_qtr()
                p_pe, p_dve, p_act = s_pe.n, s_dve.n, s_act.n
                for eng in (nc.sync, nc.tensor, nc.vector, nc.scalar, nc.gpsimd):
                    b.wait(eng, s_pe, p_pe)
                    b.wait(eng, s_dve, p_dve)
                    b.wait(eng, s_act, p_act)

            if dbg:
                nc.gpsimd.dma_start(dbg_kt[:, :], KT[:, :]).then_inc(s_out.h, 16); s_out.n += 16
                nc.gpsimd.dma_start(dbg_v[:, :], V[:, :]).then_inc(s_out.h, 16); s_out.n += 16
                nc.gpsimd.wait_ge(s_out.h, s_out.n)
                for eng in (nc.sync, nc.tensor, nc.vector, nc.scalar):
                    b.wait(eng, s_out, s_out.n)

            with ExitStack() as p2:
                sb2 = lambda n, s, d: p2.enter_context(nc.sbuf_tensor(n, s, d))
                ps2 = lambda n, s: p2.enter_context(nc.psum_tensor(n, s, F32))
                P_all = [sb2(f"P_all{i}", [128, 4 * NT * 128], BF16) for i in range(2)]
                PT_sb = [sb2(f"PT_sb{i}", [128, 512], BF16) for i in range(2)]
                nm_mac = sb2("nm_mac", [128, 4 * 8], F32)
                sums_mac = sb2("sums_mac", [128, 4 * 8], F32)
                rfac = sb2("rfac", [128, 4 * 8], F32)
                fcol = sb2("fcol", [128, 4 * 8], F32)
                nm_fin = sb2("nm_fin", [128, 4], F32)
                rowsum = sb2("rowsum", [128, 4], F32)
                recip = sb2("recip", [128, 4], F32)
                wsum = sb2("wsum", [128, 4 * 8], F32)
                ps_s = [ps2(f"ps_s{i}", [128, 1024]) for i in range(2)]
                ps_pt = [ps2(f"ps_pt{i}", [128, 512]) for i in range(2)]
                ps_o = [ps2(f"ps_o{i}", [128, 512]) for i in range(2)]

                ps_s_rd = [0, 0]
                ps_pt_rd = [(s_act, 0), (s_act, 0)]
                pt_sb_rd = [0, 0]
                ps_o_rd = [0, 0]
                pall_rd = [0, 0]
                smac = [0]

                resc_marks = {}

                def emit_score_step(u, hl, mp):
                    slot, g = u // 4, u % 4
                    ub = u % 2
                    E = slot_extent(slot)
                    M2 = (E + 7) // 8
                    h = g * 4 + hl
                    w = min(1024, (E - 8 * mp) * 128)
                    sbuf = smac[0] % 2
                    smac[0] += 1
                    b.wait(nc.tensor, s_act, ps_s_rd[sbuf])
                    ins = nc.tensor.matmul(
                        ps_s[sbuf][:, 0:min(512, w)],
                        QT_OT[:, (h * NSLOT + slot) * 128:(h * NSLOT + slot + 1) * 128],
                        KT[:, g * T + mp * 1024: g * T + mp * 1024 + min(512, w)],
                        start=True, stop=True)
                    if w > 512:
                        ins = nc.tensor.matmul(
                            ps_s[sbuf][:, 512:w],
                            QT_OT[:, (h * NSLOT + slot) * 128:(h * NSLOT + slot + 1) * 128],
                            KT[:, g * T + mp * 1024 + 512: g * T + mp * 1024 + w],
                            start=True, stop=True)
                    inc(s_pe, ins)
                    mm_mark = s_pe.n
                    if mp == M2 - 1:
                        stype = slot % 2
                        b.wait(nc.vector, s_pe, mm_mark)
                        o0 = (E - 2 - 8 * mp) * 128
                        o1 = (E - 1 - 8 * mp) * 128
                        nc.vector.tensor_tensor(
                            ps_s[sbuf][:, o0:o0 + 128], ps_s[sbuf][:, o0:o0 + 128],
                            masks_sb[:, (stype * 2) * 128:(stype * 2 + 1) * 128], op=OP.add)
                        ins = nc.vector.tensor_tensor(
                            ps_s[sbuf][:, o1:o1 + 128], ps_s[sbuf][:, o1:o1 + 128],
                            masks_sb[:, (stype * 2 + 1) * 128:(stype * 2 + 2) * 128], op=OP.add)
                        inc(s_dve, ins)
                        b.wait(nc.vector, s_dve)
                    b.wait(nc.vector, s_pe, mm_mark)
                    ins = nc.vector.reduce_max(nm_mac[:, hl * 8 + mp:hl * 8 + mp + 1],
                                               ps_s[sbuf][:, 0:w], axis=AX.X, negate=True)
                    inc(s_dve, ins)
                    b.wait(nc.scalar, s_dve)
                    ins = nc.scalar.activation(
                        P_all[ub][:, hl * NT * 128 + mp * 1024: hl * NT * 128 + mp * 1024 + w],
                        ps_s[sbuf][:, 0:w], AF.Exp,
                        bias=nm_mac[:, hl * 8 + mp:hl * 8 + mp + 1],
                        accum_out=sums_mac[:, hl * 8 + mp:hl * 8 + mp + 1])
                    inc(s_act, ins)
                    ps_s_rd[sbuf] = s_act.n

                def emit_finalize(u):
                    slot, g = u // 4, u % 4
                    ub = u % 2
                    E = slot_extent(slot)
                    M2 = (E + 7) // 8
                    nm3 = nm_mac[:, :].rearrange("p (h m) -> p h m", m=8)[:, :, 0:M2]
                    sm3 = sums_mac[:, :].rearrange("p (h m) -> p h m", m=8)[:, :, 0:M2]
                    rf3 = rfac[:, :].rearrange("p (h m) -> p h m", m=8)[:, :, 0:M2]
                    ws3 = wsum[:, :].rearrange("p (h m) -> p h m", m=8)[:, :, 0:M2]
                    fc3 = fcol[:, :].rearrange("p (h m) -> p h m", m=8)[:, :, 0:M2]
                    b.wait(nc.vector, s_act)
                    inc(s_dve, nc.vector.tensor_reduce(nm_fin[:, 0:4], nm3, axis=AX.X, op=OP.min))
                    b.wait(nc.vector, s_dve)
                    nf_b = nm_fin[:, 0:4].unsqueeze(2).broadcast_to([128, 4, M2])
                    inc(s_dve, nc.vector.tensor_tensor(rf3, nm3, nf_b, op=OP.subtract))
                    b.wait(nc.scalar, s_dve)
                    inc(s_act, nc.scalar.activation(rfac[:, :], rfac[:, :], AF.Exp, scale=-1.0))
                    b.wait(nc.vector, s_act)
                    inc(s_dve, nc.vector.tensor_tensor(ws3, rf3, sm3, op=OP.mult))
                    b.wait(nc.vector, s_dve)
                    inc(s_dve, nc.vector.tensor_reduce(rowsum[:, 0:4], ws3, axis=AX.X, op=OP.add))
                    b.wait(nc.vector, s_dve)
                    inc(s_dve, nc.vector.reciprocal(recip[:, 0:4], rowsum[:, 0:4]))
                    b.wait(nc.vector, s_dve)
                    rc_b = recip[:, 0:4].unsqueeze(2).broadcast_to([128, 4, M2])
                    inc(s_dve, nc.vector.tensor_tensor(fc3, rf3, rc_b, op=OP.mult))
                    b.wait(nc.vector, s_dve)
                    b.wait(nc.scalar, s_dve)
                    for hl in range(4):
                        for mp in range(M2):
                            w = min(1024, (E - 8 * mp) * 128)
                            sl = P_all[ub][:, hl * NT * 128 + mp * 1024: hl * NT * 128 + mp * 1024 + w]
                            if hl < 2:
                                ins = nc.vector.tensor_scalar(
                                    sl, sl, fcol[:, hl * 8 + mp:hl * 8 + mp + 1], None,
                                    op0=OP.mult)
                                inc(s_dve, ins)
                            else:
                                ins = nc.scalar.activation(
                                    sl, sl, AF.Copy, scale=fcol[:, hl * 8 + mp:hl * 8 + mp + 1])
                                inc(s_act, ins)
                    resc_marks[u] = (s_dve.n, s_act.n)

                pt_evac_n = {}

                def emit_pt_step(u, jt):
                    slot, g = u // 4, u % 4
                    ub = u % 2
                    if jt == 0:
                        b.wait(nc.tensor, s_dve, resc_marks[u][0])
                        b.wait(nc.tensor, s_act, resc_marks[u][1])
                    tb = jt % 2
                    b.wait(nc.tensor, ps_pt_rd[tb][0], ps_pt_rd[tb][1])
                    for hl in range(4):
                        ins = nc.tensor.matmul(
                            ps_pt[tb][:, hl * 128:(hl + 1) * 128],
                            P_all[ub][:, hl * NT * 128 + jt * 128: hl * NT * 128 + (jt + 1) * 128],
                            ident_sb[:, :], start=True, stop=True)
                    inc(s_pe, ins)
                    pt_mark = s_pe.n
                    if tb == 0:
                        b.wait(nc.scalar, s_pe, pt_mark)
                        b.wait(nc.scalar, s_pe, pt_sb_rd[tb])
                        inc(s_act, nc.scalar.activation(PT_sb[tb][:, :], ps_pt[tb][:, :], AF.Copy))
                        ps_pt_rd[tb] = (s_act, s_act.n)
                        pt_evac_n[(u, jt)] = (s_act, s_act.n)
                    else:
                        b.wait(nc.vector, s_pe, pt_mark)
                        b.wait(nc.vector, s_pe, pt_sb_rd[tb])
                        inc(s_dve, nc.vector.tensor_copy(PT_sb[tb][:, :], ps_pt[tb][:, :]))
                        ps_pt_rd[tb] = (s_dve, s_dve.n)
                        pt_evac_n[(u, jt)] = (s_dve, s_dve.n)

                def emit_pvmm_step(u, jt):
                    slot, g = u // 4, u % 4
                    ub = u % 2
                    E = slot_extent(slot)
                    tb = jt % 2
                    if jt == 0:
                        b.wait(nc.tensor, s_act, ps_o_rd[ub])
                    ev_sem, ev_n = pt_evac_n.pop((u, jt))
                    b.wait(nc.tensor, ev_sem, ev_n)
                    ins = nc.tensor.matmul(ps_o[ub][:, :],
                                           V[:, (jt * NG + g) * 128:(jt * NG + g + 1) * 128],
                                           PT_sb[tb][:, :],
                                           start=(jt == 0), stop=(jt == E - 1))
                    inc(s_pe, ins)
                    pt_sb_rd[tb] = s_pe.n
                    if jt == E - 1:
                        pall_rd[ub] = s_pe.n
                        pv_mark = s_pe.n
                        b.wait(nc.scalar, s_pe, pv_mark)
                        dst = QT_v[:, g * 4:(g + 1) * 4, slot * 128:(slot + 1) * 128]
                        src = ps_o[ub][:, :].rearrange("p (hl n) -> p hl n", n=128)
                        inc(s_act, nc.scalar.activation(dst, src, AF.Copy))
                        ps_o_rd[ub] = s_act.n

                def unit_steps(u):
                    slot = u // 4
                    E = slot_extent(slot)
                    M2 = (E + 7) // 8
                    return [(hl, mp) for hl in range(4) for mp in range(M2)], E

                conv_buf2 = [p2.enter_context(nc.sbuf_tensor(f"cvd{i}", [128, CONV_COLS], BF16))
                             for i in range(2)]
                for u in range(64):
                    emit_conversions(conv_buf2, 7)
                    ssteps, _ = unit_steps(u)
                    psteps = unit_steps(u - 1)[1] if u >= 1 else 0
                    ub = u % 2
                    b.wait(nc.scalar, s_pe, pall_rd[ub])
                    n = max(len(ssteps), psteps)
                    pv_done = 0
                    for k in range(n):
                        if k < len(ssteps):
                            emit_score_step(u, *ssteps[k])
                        if k < psteps:
                            emit_pt_step(u - 1, k)
                        if k >= 1 and pv_done < min(k, psteps):
                            emit_pvmm_step(u - 1, pv_done)
                            pv_done += 1
                    while pv_done < psteps:
                        emit_pvmm_step(u - 1, pv_done)
                        pv_done += 1
                    emit_finalize(u)
                for jt in range(unit_steps(63)[1]):
                    emit_pt_step(63, jt)
                    if jt >= 1:
                        emit_pvmm_step(63, jt - 1)
                emit_pvmm_step(63, unit_steps(63)[1] - 1)
                assert conv_state["i"] == len(conv_jobs), conv_state["i"]

                p_pe, p_dve, p_act = s_pe.n, s_dve.n, s_act.n
                for eng in (nc.sync, nc.tensor, nc.vector, nc.scalar, nc.gpsimd):
                    b.wait(eng, s_pe, p_pe)
                    b.wait(eng, s_dve, p_dve)
                    b.wait(eng, s_act, p_act)

        if dbg:
            nc.gpsimd.dma_start(dbg_ot[:, :], QT_OT[:, :]).then_inc(s_out.h, 16); s_out.n += 16
            nc.gpsimd.wait_ge(s_out.h, s_out.n)
            for eng in (nc.sync, nc.tensor, nc.vector, nc.scalar):
                b.wait(eng, s_out, s_out.n)

        with ExitStack() as p25:
            sb5 = lambda n, s, d: p25.enter_context(nc.sbuf_tensor(n, s, d))
            wo_sb = sb5("wo_sb", [128, NKC * 2048], BF16)
            x_buf = [sb5(f"x25_{i}", [128, D], F32) for i in range(2)]
            res_sb = [sb5(f"res_sb{i}", [128, 512], F32) for i in range(2)]
            ps25 = [p25.enter_context(nc.psum_tensor(f"ps25_{i}", [128, 512], F32))
                    for i in range(4)]
            b.wait(nc.sync, s_cw[0])
            b.wait(nc.sync, s_cw[1])
            nc.sync.dma_start(wo_sb[:, :], wo_s[:, :]).then_inc(s_w.h, 16); s_w.n += 16
            wo_ready = s_w.n
            b.wait(nc.tensor, s_w, wo_ready)
            x_rd = [0, 0]
            ps25_rd = [0, 0, 0, 0]
            for slot in range(NSLOT):
                xbuf = slot % 2
                b.wait(nc.sync, s_dve, x_rd[xbuf])
                nc.sync.dma_start(x_buf[xbuf][:, :],
                                  xq[slot * 128:(slot + 1) * 128, :]).then_inc(s_x[xbuf].h, 16)
                s_x[xbuf].n += 16
                for dc in range(4):
                    pq = (slot * 4 + dc) % 4
                    b.wait(nc.tensor, s_dve, ps25_rd[pq])
                    for kc in range(NKC):
                        ins = nc.tensor.matmul(
                            ps25[pq][:, :],
                            QT_OT[:, (kc * NSLOT + slot) * 128:(kc * NSLOT + slot + 1) * 128],
                            wo_sb[:, kc * 2048 + dc * 512: kc * 2048 + (dc + 1) * 512],
                            start=(kc == 0), stop=(kc == NKC - 1))
                    inc(s_pe, ins)
                    wo_mark = s_pe.n
                    rb = (slot * 4 + dc) % 2
                    b.wait(nc.vector, s_pe, wo_mark)
                    b.wait(nc.vector, s_x[xbuf])
                    b.wait(nc.vector, s_r1[rb])
                    ins = nc.vector.tensor_tensor(res_sb[rb][:, :], ps25[pq][:, :],
                                                  x_buf[xbuf][:, dc * 512:(dc + 1) * 512], op=OP.add)
                    inc(s_dve, ins)
                    ps25_rd[pq] = s_dve.n
                    x_rd[xbuf] = s_dve.n
                    b.wait(nc.sync, s_dve, s_dve.n)
                    nc.sync.dma_start(res1[slot * 128:(slot + 1) * 128, dc * 512:(dc + 1) * 512],
                                      res_sb[rb][:, :]).then_inc(s_r1[rb].h, 16)
                    s_r1[rb].n += 16
            p_pe, p_dve, p_act = s_pe.n, s_dve.n, s_act.n
            for eng in (nc.sync, nc.tensor, nc.vector, nc.scalar, nc.gpsimd):
                b.wait(eng, s_pe, p_pe)
                b.wait(eng, s_dve, p_dve)
                b.wait(eng, s_act, p_act)
                b.wait(eng, s_r1[0])
                b.wait(eng, s_r1[1])

    with ExitStack() as p3:
        sb3 = lambda n, s, d: p3.enter_context(nc.sbuf_tensor(n, s, d))
        gated = sb3("gated", [128, 32 * 512], BF16)
        out_acc = sb3("out_acc", [128, 4 * 2048], F32)
        xnT_g = sb3("xnT_g", [128, NKC * 512], BF16)
        x_buf = [sb3(f"x3_{i}", [128, D], F32) for i in range(2)]
        xn_buf = [sb3(f"xn3{i}", [128, D], BF16) for i in range(2)]
        wgu_buf = [sb3(f"wgu{i}", [128, 2 * 2048], BF16) for i in range(4)]
        wd_buf = [sb3(f"wd{i}", [128, 16 * 512], BF16) for i in range(2)]
        r1_buf = [sb3(f"r1b{i}", [128, 4 * 512], F32) for i in range(2)]
        gate_bf = [sb3(f"gate_bf{i}", [128, 512], BF16) for i in range(2)]
        bn_tmp = sb3("bn3", [128, 24], F32)
        mnv = sb3("mnv3", [128, 2], F32)
        rstdc = sb3("rstd3", [128, 8], F32)
        ps_g = [p3.enter_context(nc.psum_tensor(f"ps3g{i}", [128, 512], F32)) for i in range(2)]
        ps_u = [p3.enter_context(nc.psum_tensor(f"ps3u{i}", [128, 512], F32)) for i in range(2)]
        ps_d = [p3.enter_context(nc.psum_tensor(f"ps3d{i}", [128, 512], F32)) for i in range(4)]

        x_rd = [0, 0]
        xn_rd_l = [0, 0]
        xnT_rd = 0
        ps_g_rd = [0, 0]
        ps_u_rd = [0, 0]
        gate_rd = [0, 0]
        wgu_rd = [0, 0, 0, 0]
        wgu_ready = [0, 0, 0, 0]
        wd_rd = [0, 0]
        wd_ready = [0, 0]
        r1_rd = [0, 0]
        ps_d_rd = [0, 0, 0, 0]
        st_done = [0, 0]

        for gi in range(4):
            for i in range(4):
                slot = gi * 4 + i
                xbuf = i % 2
                b.wait(nc.sync, s_dve, x_rd[xbuf])
                nc.sync.dma_start(x_buf[xbuf][:, :],
                                  xq[slot * 128:(slot + 1) * 128, :]).then_inc(s_x[xbuf].h, 16)
                s_x[xbuf].n += 16
                b.wait(nc.vector, s_x[xbuf])
                xn_mark = emit_ln(x_buf[xbuf][:, :], xn_buf[xbuf][:, :], bn_tmp, mnv, rstdc, xn_rd_l[xbuf])
                x_rd[xbuf] = s_dve.n
                b.wait(nc.tensor, s_dve, xn_mark)
                for bi in range(4):
                    pb = bi % 2
                    b.wait(nc.tensor, s_act, ps_g_rd[pb])
                    b.wait(nc.tensor, s_pe, xnT_rd)
                    for j in range(4):
                        kc = bi * 4 + j
                        ins = nc.tensor.matmul(ps_g[pb][:, j * 128:(j + 1) * 128],
                                               xn_buf[xbuf][:, kc * 128:(kc + 1) * 128],
                                               ident_sb[:, :], start=True, stop=True)
                    inc(s_pe, ins)
                    b.wait(nc.scalar, s_pe)
                    dst = xnT_g[:, :].rearrange("p (kc n) -> p kc n", n=512)[
                        :, bi * 4:(bi + 1) * 4, i * 128:(i + 1) * 128]
                    src = ps_g[pb][:, :].rearrange("p (j n) -> p j n", n=128)
                    inc(s_act, nc.scalar.activation(dst, src, AF.Copy))
                    ps_g_rd[pb] = s_act.n
                xn_rd_l[xbuf] = s_pe.n
            xnT_mark = s_act.n

            for half in range(2):
                for sp in range(16):
                    wb = sp % 2
                    hc0 = half * 32 + sp * 2
                    b.wait(nc.sync, s_pe, wgu_rd[wb])
                    nc.sync.dma_start(wgu_buf[wb][:, :],
                                      wg_s[:, hc0 * 2048:(hc0 + 2) * 2048]).then_inc(s_wg[wb].h, 16)
                    s_wg[wb].n += 16
                    wgu_ready[wb] = s_wg[wb].n
                    b.wait(nc.sync, s_pe, wgu_rd[2 + wb])
                    nc.sync.dma_start(wgu_buf[2 + wb][:, :],
                                      wu_s[:, hc0 * 2048:(hc0 + 2) * 2048]).then_inc(s_wu[wb].h, 16)
                    s_wu[wb].n += 16
                    wgu_ready[2 + wb] = s_wu[wb].n
                    for par in range(2):
                        hch = sp * 2 + par
                        pg2 = hch % 2
                        b.wait(nc.tensor, s_act, xnT_mark)
                        b.wait(nc.tensor, s_wg[wb], wgu_ready[wb])
                        b.wait(nc.tensor, s_act, ps_g_rd[pg2])
                        for kc in range(NKC):
                            ins = nc.tensor.matmul(
                                ps_g[pg2][:, :],
                                wgu_buf[wb][:, par * 2048 + kc * 128: par * 2048 + (kc + 1) * 128],
                                xnT_g[:, kc * 512:(kc + 1) * 512],
                                start=(kc == 0), stop=(kc == NKC - 1))
                        inc(s_pe, ins)
                        wgu_rd[wb] = s_pe.n
                        g_mark = s_pe.n
                        b.wait(nc.tensor, s_wu[wb], wgu_ready[2 + wb])
                        b.wait(nc.tensor, s_dve, ps_u_rd[pg2])
                        for kc in range(NKC):
                            ins = nc.tensor.matmul(
                                ps_u[pg2][:, :],
                                wgu_buf[2 + wb][:, par * 2048 + kc * 128: par * 2048 + (kc + 1) * 128],
                                xnT_g[:, kc * 512:(kc + 1) * 512],
                                start=(kc == 0), stop=(kc == NKC - 1))
                        inc(s_pe, ins)
                        wgu_rd[2 + wb] = s_pe.n
                        u_mark = s_pe.n
                        b.wait(nc.scalar, s_pe, g_mark)
                        b.wait(nc.scalar, s_dve, gate_rd[pg2])
                        inc(s_act, nc.scalar.activation(gate_bf[pg2][:, :], ps_g[pg2][:, :], AF.Silu))
                        ps_g_rd[pg2] = s_act.n
                        b.wait(nc.vector, s_act, ps_g_rd[pg2])
                        b.wait(nc.vector, s_pe, u_mark)
                        ins = nc.vector.tensor_tensor(gated[:, hch * 512:(hch + 1) * 512],
                                                      gate_bf[pg2][:, :], ps_u[pg2][:, :], op=OP.mult)
                        inc(s_dve, ins)
                        ps_u_rd[pg2] = s_dve.n
                        gate_rd[pg2] = s_dve.n
                gated_mark = s_dve.n
                if dbg == 2 and gi == 0 and half == 0:
                    b.wait(nc.gpsimd, s_dve, gated_mark)
                    b.wait(nc.gpsimd, s_act, s_act.n)
                    nc.gpsimd.dma_start(dbg_gated[:, :], gated[:, :]).then_inc(s_out.h, 16); s_out.n += 16
                    nc.gpsimd.dma_start(dbg_xnt[:, :], xnT_g[:, :]).then_inc(s_out.h, 16); s_out.n += 16
                    nc.gpsimd.wait_ge(s_out.h, s_out.n)
                    for eng in (nc.sync, nc.tensor, nc.vector, nc.scalar):
                        b.wait(eng, s_out, s_out.n)
                for dc in range(4):
                    if half == 0:
                        rb = dc % 2
                        b.wait(nc.sync, s_dve, r1_rd[rb])
                        r1v = res1.rearrange("(s p) d -> p s d", p=128)
                        nc.sync.dma_start(
                            r1_buf[rb][:, :].rearrange("p (tt n) -> p tt n", n=512),
                            r1v[:, gi * 4:(gi + 1) * 4, dc * 512:(dc + 1) * 512]).then_inc(s_r1[rb].h, 16)
                        s_r1[rb].n += 16
                    for sub in range(2):
                        wb = sub
                        sl0 = ((half * 4 + dc) * 2 + sub) * 8192
                        b.wait(nc.sync, s_pe, wd_rd[wb])
                        nc.sync.dma_start(wd_buf[wb][:, :],
                                          wd_s[:, sl0:sl0 + 8192]).then_inc(s_wd[wb].h, 16)
                        s_wd[wb].n += 16
                        wd_ready[wb] = s_wd[wb].n
                        b.wait(nc.tensor, s_dve, gated_mark)
                        b.wait(nc.tensor, s_wd[wb], wd_ready[wb])
                        for tt in range(4):
                            if sub == 0:
                                b.wait(nc.tensor, s_dve, ps_d_rd[tt])
                            for hcs in range(16):
                                hch = sub * 16 + hcs
                                ins = nc.tensor.matmul(
                                    ps_d[tt][:, :],
                                    gated[:, hch * 512 + tt * 128: hch * 512 + tt * 128 + 128],
                                    wd_buf[wb][:, hcs * 512:(hcs + 1) * 512],
                                    start=(sub == 0 and hcs == 0), stop=(sub == 1 and hcs == 15))
                            inc(s_pe, ins)
                        wd_rd[wb] = s_pe.n
                    down_mark = s_pe.n
                    b.wait(nc.vector, s_pe, down_mark)
                    for tt in range(4):
                        dstc = out_acc[:, tt * 2048 + dc * 512: tt * 2048 + (dc + 1) * 512]
                        if half == 0:
                            rb = dc % 2
                            b.wait(nc.vector, s_r1[rb])
                            b.wait(nc.vector, s_st[tt % 2], st_done[tt % 2])
                            ins = nc.vector.tensor_tensor(
                                dstc, ps_d[tt][:, :],
                                r1_buf[rb][:, tt * 512:(tt + 1) * 512], op=OP.add)
                            r1_rd[rb] = s_dve.n + 1
                        else:
                            ins = nc.vector.tensor_tensor(dstc, dstc, ps_d[tt][:, :], op=OP.add)
                        inc(s_dve, ins)
                        ps_d_rd[tt] = s_dve.n
            acc_mark = s_dve.n
            b.wait(nc.sync, s_dve, acc_mark)
            for tt in range(4):
                slot = gi * 4 + tt
                nc.sync.dma_start(out[slot * 128:(slot + 1) * 128, :],
                                  out_acc[:, tt * 2048:(tt + 1) * 2048]).then_inc(s_st[tt % 2].h, 16)
                s_st[tt % 2].n += 16
            st_done = [s_st[0].n, s_st[1].n]

        nc.sync.wait_ge(s_st[0].h, s_st[0].n)
        nc.sync.wait_ge(s_st[1].h, s_st[1].n)
        if s_out.n:
            nc.sync.wait_ge(s_out.h, s_out.n)

    return b


def _host_tables(p):
    own = own_tiles_for_parity(p)
    d = np.arange(64, dtype=np.float64)
    ang = 10000.0 ** (-d / 64.0)
    rows = np.arange(128, dtype=np.float64)[:, None]
    kcos = np.empty((128, NT * 64), np.float32)
    ksin = np.empty((128, NT * 64), np.float32)
    for t in range(NT):
        rad = (t * 128 + rows) * ang[None, :]
        kcos[:, t * 64:(t + 1) * 64] = np.cos(rad).astype(np.float32)
        ksin[:, t * 64:(t + 1) * 64] = np.sin(rad).astype(np.float32)
    qcos = np.empty((128, NSLOT * 64), np.float32)
    qsin = np.empty((128, NSLOT * 64), np.float32)
    for s in range(NSLOT):
        rad = (own[s] * 128 + rows) * ang[None, :]
        qcos[:, s * 64:(s + 1) * 64] = np.cos(rad).astype(np.float32)
        qsin[:, s * 64:(s + 1) * 64] = np.sin(rad).astype(np.float32)
    tri = np.where(np.arange(128)[:, None] >= np.arange(128)[None, :], 0.0, NEG).astype(np.float32)
    full = np.full((128, 128), NEG, np.float32)
    zero = np.zeros((128, 128), np.float32)
    msets = [tri, full, zero, tri] if p == 0 else [zero, tri, tri, full]
    masks = np.concatenate(msets, axis=1).astype(np.float32)
    return own, kcos, ksin, qcos, qsin, masks


def _relayout_weights(wq, wk, wv, wo, wg, wu, wd):
    wq_r = np.ascontiguousarray(
        wq.reshape(16, 128, 4, 512).transpose(1, 2, 0, 3).reshape(128, 4 * 16 * 512))
    wk_r = np.ascontiguousarray(wk.reshape(16, 128, 512).transpose(1, 0, 2).reshape(128, 16 * 512))
    wv_r = np.ascontiguousarray(wv.reshape(16, 128, 512).transpose(1, 0, 2).reshape(128, 16 * 512))
    wo_r = np.ascontiguousarray(wo.reshape(16, 128, 2048).transpose(1, 0, 2).reshape(128, 16 * 2048))
    wg_r = np.ascontiguousarray(
        wg.reshape(64, 128, 16, 128).transpose(3, 0, 2, 1).reshape(128, 64 * 2048))
    wu_r = np.ascontiguousarray(
        wu.reshape(64, 128, 16, 128).transpose(3, 0, 2, 1).reshape(128, 64 * 2048))
    wd_r = np.ascontiguousarray(
        wd.T.reshape(2, 2, 16, 128, 4, 512).transpose(3, 0, 4, 1, 2, 5).reshape(128, 16 * 8192))
    return wq_r, wk_r, wv_r, wo_r, wg_r, wu_r, wd_r


_BUILDER_CACHE = {}


def _get_builder():
    if "b" not in _BUILDER_CACHE:
        _BUILDER_CACHE["b"] = build_program(dbg=False)
    return _BUILDER_CACHE["b"]


def kernel(L0_attn_x, L0_attn_wq, L0_attn_wk, L0_attn_wv, L0_attn_wo,
           L0_attn_gq, L0_attn_gk, L0_ffn_w_gate, L0_ffn_w_up, L0_ffn_w_down):
    x = np.ascontiguousarray(np.asarray(L0_attn_x, dtype=np.float32))
    wq_np = np.asarray(L0_attn_wq, np.float32)
    wk_np = np.asarray(L0_attn_wk, np.float32)
    wv_np = np.asarray(L0_attn_wv, np.float32)
    wo_np = np.asarray(L0_attn_wo, np.float32)
    gq_np = np.asarray(L0_attn_gq, np.float32).reshape(1, D)
    gk_np = np.asarray(L0_attn_gk, np.float32).reshape(1, 512)
    wg_np = np.asarray(L0_ffn_w_gate, np.float32)
    wu_np = np.asarray(L0_ffn_w_up, np.float32)
    wd_np = np.asarray(L0_ffn_w_down, np.float32)
    wq_r, wk_r, wv_r, wo_r, wg_r, wu_r, wd_r = _relayout_weights(
        wq_np, wk_np, wv_np, wo_np, wg_np, wu_np, wd_np)
    gqb_np = np.ascontiguousarray(np.broadcast_to(gq_np, (128, D)))
    gkb_np = np.ascontiguousarray(np.broadcast_to(gk_np, (128, 512)))
    ident_np = np.eye(128, dtype=ml_dtypes.bfloat16)
    consts_np = np.ascontiguousarray(np.broadcast_to(
        np.array([[EPS_LN, EPS_RMS, LNMULT, 0.0]], np.float32), (128, 4)))

    bb = _get_builder()
    in_maps = []
    owns = []
    for c in range(8):
        bidx, p = c // 2, c % 2
        own, kcos_np, ksin_np, qcos_np, qsin_np, masks_np = _host_tables(p)
        owns.append(own)
        xb_np = np.ascontiguousarray(x[bidx])
        xq_np = np.ascontiguousarray(
            xb_np.reshape(NT, 128, D)[own].reshape(NSLOT * 128, D))
        in_maps.append({
            "xb": xb_np, "xq": xq_np,
            "wq_r": wq_r, "wk_r": wk_r, "wv_r": wv_r, "wo_r": wo_r,
            "wg_r": wg_r, "wu_r": wu_r, "wd_r": wd_r,
            "gqb": gqb_np, "gkb": gkb_np,
            "kcos": kcos_np, "ksin": ksin_np,
            "qcos": qcos_np, "qsin": qsin_np,
            "ident": ident_np, "masks": masks_np, "consts": consts_np,
        })

    res = run_bass_kernel_spmd(bb.nc, in_maps, core_ids=list(range(8)))
    outp = np.empty((B, T, D), np.float32)
    for c in range(8):
        bidx = c // 2
        o = np.asarray(res.results[c]["out"]).reshape(NSLOT, 128, D)
        for s, t in enumerate(owns[c]):
            outp[bidx, t * 128:(t + 1) * 128, :] = o[s]
    return outp
